# version 13
# baseline (speedup 1.0000x reference)
import contextlib
import numpy as np
import ml_dtypes
import concourse.bass as bass
import concourse.mybir as mybir
from concourse.bass_utils import run_bass_kernel_spmd

F32 = mybir.dt.float32
BF16 = mybir.dt.bfloat16
AF = mybir.ActivationFunctionType
ALU = mybir.AluOpType
AX = mybir.AxisListType
BF = ml_dtypes.bfloat16

TOK = 2048
NT = 512
TTN = 4
D = 1024
DCH = 8
SEQ = 4096
NKT = 32
NHP = 8
LN_EPS = 1e-5
GROUPS = [[0, 1], [2, 3], [4, 5], [6, 7]]
ARNW = 24832

ENGS = ("sync", "tensor", "scalar", "vector", "gpsimd")
SEMOF = {"tensor": "mm", "scalar": "sc", "vector": "ev", "gpsimd": "gp"}
CHAIN = {"scalar", "vector", "gpsimd"}


class Sem:
    def __init__(self, h):
        self.h = h
        self.v = 0


class Ctx:
    pass


class Sch:

    def __init__(self, c):
        self.c = c
        self.q = {e: [] for e in ENGS}
        self.last = {e: None for e in ENGS}

    def op(self, eng, fn, waits=(), inc=None, chain=True, dma=None):
        c = self.c
        w = [x for x in waits if x is not None and x[1] is not None]
        incs = []
        ret = None
        if eng == "gpsimd" and dma is not None:
            c.sem[dma].v += 16
            ret = c.sem[dma].v
            incs = [(dma, 16)]
        elif eng in CHAIN:
            if chain and self.last[eng] is not None:
                w.append(self.last[eng])
            sn = inc if inc is not None else SEMOF[eng]
            c.sem[sn].v += 1
            ret = c.sem[sn].v
            incs = [(sn, 1)]
            self.last[eng] = (sn, ret)
        elif eng == "tensor":
            if inc is not None:
                c.sem[inc].v += 1
                ret = c.sem[inc].v
                incs = [(inc, 1)]
        elif eng == "sync":
            if dma is not None:
                c.sem[dma].v += 16
                ret = c.sem[dma].v
                incs = [(dma, 16)]
        self.q[eng].append((w, fn, incs))
        return ret

    _n = [0]

    def play(self, name):
        c = self.c
        Sch._n[0] += 1
        with c.nc.Block(f"{name}_{Sch._n[0]}") as blk:
            for eng in ENGS:
                if not self.q[eng]:
                    continue

                def runner(e, lst=tuple(self.q[eng])):
                    seen = {}
                    for w, fn, incs in lst:
                        for sn, val in w:
                            if seen.get(sn, -1) >= val:
                                continue
                            e.wait_ge(c.sem[sn].h, val)
                            seen[sn] = val
                        ins = fn(e)
                        for sn, n in incs:
                            ins.then_inc(c.sem[sn].h, n)

                getattr(blk, eng)(runner)
        self.q = {e: [] for e in ENGS}
        self.last = {e: None for e in ENGS}



def host_prep(inp):
    h = {}
    f32 = np.float32

    def c(a):
        return np.ascontiguousarray(np.asarray(a, f32))

    h["x_fm"] = c(np.transpose(np.asarray(inp["L0_attn_x"], f32), (0, 2, 1)))
    qkv = np.asarray(inp["L0_attn_qkv_w"], f32)
    h["l0_qw"] = c(qkv[:, 0:1024])
    h["l0_kw"] = c(qkv[:, 1024:2048])
    h["l0_vw"] = c(qkv[:, 2048:3072])
    qb = np.asarray(inp["L0_attn_qkv_b"], f32)
    h["l0_bq"] = c(qb[0:1024].reshape(8, 128).T)
    h["l0_bk"] = c(qb[1024:2048].reshape(8, 128).T)
    h["l0_bv_row"] = c(qb[2048:3072].reshape(1, 1024))
    h["l0_pw"] = c(inp["L0_attn_proj_w"])
    h["l0_pb"] = c(np.asarray(inp["L0_attn_proj_b"], f32).reshape(8, 128).T)
    h["l0_wg"] = c(inp["L0_ffn_Wg"])
    h["l0_bg16"] = c(np.tile(np.asarray(inp["L0_ffn_bg"], f32).reshape(1, 8), (1, 16)))
    h["l0_w1"] = c(inp["L0_ffn_W1"])
    h["l0_b1"] = c(np.asarray(inp["L0_ffn_b1"], f32).reshape(8, 32, 128)
                   .transpose(2, 0, 1).reshape(128, 256))
    h["l0_w2"] = c(inp["L0_ffn_W2"])
    h["l0_b2"] = c(inp["L0_ffn_b2"])
    h["l1_qw"] = c(np.asarray(inp["L1_attn_Wq"], f32).T)
    h["l1_kw"] = c(np.asarray(inp["L1_attn_Wk"], f32).T)
    h["l1_vw"] = c(np.asarray(inp["L1_attn_Wv"], f32).T)
    h["l1_pw"] = c(np.asarray(inp["L1_attn_Wo"], f32).T)
    h["l1_bq"] = c(np.asarray(inp["L1_attn_bq"], f32).reshape(8, 128).T)
    h["l1_bk"] = c(np.asarray(inp["L1_attn_bk"], f32).reshape(8, 128).T)
    h["l1_bv_row"] = c(np.asarray(inp["L1_attn_bv"], f32).reshape(1, 1024))
    h["l1_pb"] = c(np.asarray(inp["L1_attn_bo"], f32).reshape(8, 128).T)
    h["l1_wg"] = c(np.asarray(inp["L1_ffn_gate_w"], f32).T)
    h["l1_bg16"] = c(np.tile(np.asarray(inp["L1_ffn_beta"], f32).reshape(1, 8), (1, 16)))
    h["l1_w1"] = c(inp["L1_ffn_gate_proj_w"])
    h["l1_uw"] = c(inp["L1_ffn_up_proj_w"])
    h["l1_w2"] = c(inp["L1_ffn_down_proj_w"])
    h["l1_sg"] = c(np.asarray(inp["L1_ffn_shared_gate_w"], f32)[0])
    h["l1_su"] = c(np.asarray(inp["L1_ffn_shared_up_w"], f32)[0])
    h["l1_sd"] = c(np.asarray(inp["L1_ffn_shared_down_w"], f32)[0])
    h["ident"] = np.eye(128, dtype=f32)
    sel = np.zeros((8, 1024), f32)
    for e in range(8):
        sel[e, e * 128:(e + 1) * 128] = 1.0
    h["sel"] = sel
    return h


def core_thr(core):
    half = core % 2
    thr = np.zeros((TTN, NKT, 128), np.float32)
    i = np.arange(128, dtype=np.float32)
    for qt in range(TTN):
        qb = half * TOK + qt * NT
        for kt in range(NKT):
            thr[qt, kt] = i - qb + kt * 128
    return np.ascontiguousarray(thr.transpose(2, 0, 1).reshape(128, TTN * NKT))



def make_ctx(nc):
    c = Ctx()
    c.nc = nc
    a = nc.alloc_sbuf_tensor
    c.A = a("A", [128, DCH * TOK], F32)
    c.XB = a("XBuf", [128, DCH * TOK], BF16)
    c.SCR = a("SCR", [128, 16384], BF16)
    c.STG = a("STG", [128, 4096], F32)
    c.ARN = a("ARN", [128, ARNW], BF16)
    c.ONES_B = a("ONES_B", [128, 128], BF16)
    c.ONES_F = a("ONES_F", [128, 128], F32)
    c.IDENT = a("IDENT", [128, 128], F32)
    c.IDENTB = a("IDENTB", [128, 128], BF16)
    c.IOTA = a("IOTA", [128, 512], F32)
    c.IOTA2 = a("IOTA2", [128, 1024], mybir.dt.float16)
    c.THR = a("THR", [128, TTN * NKT], F32)
    c.SEL = a("SEL_sb", [8, 1024], BF16)
    c.B1 = a("B1", [128, 256], F32)
    c.B2BF = a("B2BF", [8, 1024], BF16)
    c.BQR = a("BQR", [128, 8], F32)
    c.BQ8 = a("BQ8", [128, 8], F32)
    c.BK = a("BK", [128, 8], F32)
    c.BP = a("BP", [128, 8], F32)
    c.EPS = a("EPS", [128, 1], F32)
    c.PSA = nc.alloc_psum_tensor("PSA", [128, 1024], F32)
    c.PSB = nc.alloc_psum_tensor("PSB", [128, 1024], F32)
    c.PSC = nc.alloc_psum_tensor("PSC", [128, 1024], F32)
    c.PSD = nc.alloc_psum_tensor("PSD", [128, 512], F32)
    c.PSE = nc.alloc_psum_tensor("PSE", [128, 512], F32)
    c.dram = {}
    c.input_names = set()
    return c


def din(c, name, shape, dtype=F32):
    if name not in c.dram:
        c.dram[name] = c.nc.dram_tensor(name, list(shape), dtype, kind="ExternalInput")
        c.input_names.add(name)
    return c.dram[name]


def dout(c, name, shape, dtype=F32):
    if name not in c.dram:
        c.dram[name] = c.nc.dram_tensor(name, list(shape), dtype, kind="ExternalOutput")
    return c.dram[name]


def dint(c, name, shape, dtype=BF16):
    if name not in c.dram:
        c.dram[name] = c.nc.dram_tensor(name, list(shape), dtype, kind="Internal")
    return c.dram[name]


def at(c, ch, tt):
    return c.A[:, ch * TOK + tt * NT: ch * TOK + tt * NT + NT]


def xt(c, ch, tt):
    return c.XB[:, ch * TOK + tt * NT: ch * TOK + tt * NT + NT]


def arn(c, off, w):
    return c.ARN[:, off: off + w]


def stg(c, off, w):
    return c.STG[:, off: off + w]



def ph_init(c):
    q = Sch(c)
    ih = din(c, "ident", [128, 128])
    th = din(c, "thr", [128, TTN * NKT])
    sh = din(c, "sel", [8, 1024])
    b1h = din(c, "l0_b1", [128, 256])
    b2h = din(c, "l0_b2", [8, 1024])
    mi = q.op("sync", lambda s: s.dma_start(out=c.IDENT[:], in_=ih.ap()), dma="ld0")
    q.op("sync", lambda s: s.dma_start(out=c.THR[:], in_=th.ap()), dma="ld0")
    q.op("sync", lambda s: s.dma_start(out=c.B1[:], in_=b1h.ap()), dma="ld0")
    q.op("gpsimd", lambda g: g.dma_start(out=c.SEL[:], in_=sh.ap()), dma="sw0")
    m = q.op("gpsimd", lambda g: g.dma_start(out=c.B2BF[:], in_=b2h.ap()), dma="sw0")
    q.op("gpsimd", lambda g: g.memset(c.ONES_B[:], 1.0))
    q.op("gpsimd", lambda g: g.memset(c.ONES_F[:], 1.0))
    q.op("gpsimd", lambda g: g.memset(c.EPS[:], LN_EPS))
    q.op("gpsimd", lambda g: g.iota(c.IOTA[:], [[1, 512]], channel_multiplier=0,
                                    allow_small_or_imprecise_dtypes=True))
    q.op("gpsimd", lambda g: g.iota(c.IOTA2[:, 0:512], [[1, 512]],
                                    channel_multiplier=0,
                                    allow_small_or_imprecise_dtypes=True))
    q.op("gpsimd", lambda g: g.iota(c.IOTA2[:, 512:1024], [[1, 512]],
                                    channel_multiplier=0,
                                    allow_small_or_imprecise_dtypes=True))
    q.op("gpsimd", lambda g: g.tensor_copy(out=c.IDENTB[:], in_=c.IDENT[:]),
         waits=[("sw0", m), ("ld0", c.sem["ld0"].v)])
    q.play("init")


def ph_load_x(c, name="x_fm_slice"):
    q = Sch(c)
    xh = din(c, name, [D, TOK])
    m = q.op("sync", lambda s: s.dma_start(
        out=c.A.rearrange("p (a t) -> p a t", a=DCH),
        in_=xh.rearrange("(a p) t -> p a t", p=128)), dma="ld0")
    q.op("scalar", lambda s: s.copy(out=c.XB[:], in_=c.A[:]), waits=[("ld0", m)])
    q.play("loadx")


def ph_store_stream(c, name="stream_out"):
    q = Sch(c)
    oh = dout(c, name, [D, TOK])
    m = q.op("sync", lambda s: s.dma_start(
        out=oh.rearrange("(a p) t -> p a t", p=128),
        in_=c.A.rearrange("p (a t) -> p a t", a=DCH)), dma="st0")
    q.op("sync", lambda s: s.wait_ge(c.sem["st0"].h, m))
    q.play("store_" + name)


def ph_ln(c, write_xb=True):
    q = Sch(c)
    MEANB, RINV = stg(c, 0, 512), stg(c, 512, 512)
    VAR, SD, DD = stg(c, 1024, 512), stg(c, 1536, 512), stg(c, 2048, 512)
    castm, statm, consm = {}, {}, {}
    for tt in range(TTN):
        for ch in range(DCH):
            k = tt * DCH + ch
            sl = (k % 2) * 512
            w = [("mm", statm[k - 2])] if k >= 2 else []
            q.op("scalar", lambda s, ch=ch, tt=tt, sl=sl:
                 s.copy(out=arn(c, sl, 512), in_=at(c, ch, tt)), waits=w)
            castm[k] = q.op("scalar", lambda s, ch=ch, tt=tt, sl=sl:
                            s.activation(out=arn(c, 1024 + sl, 512),
                                         in_=at(c, ch, tt), func=AF.Square))
            w2 = [("sc", castm[k])]
            if ch == 0 and tt >= 1:
                w2.append(("ev", consm[tt - 1]))
            q.op("tensor", lambda t, sl=sl, ch=ch:
                 t.matmul(c.PSA[:, 0:512], c.ONES_B[:], arn(c, sl, 512),
                          start=(ch == 0), stop=(ch == DCH - 1)), waits=w2)
            statm[k] = q.op("tensor", lambda t, sl=sl, ch=ch:
                            t.matmul(c.PSA[:, 512:1024], c.ONES_B[:],
                                     arn(c, 1024 + sl, 512),
                                     start=(ch == 0), stop=(ch == DCH - 1)),
                            inc="mm")
        q.op("vector", lambda v: v.tensor_scalar(
            out=MEANB, in0=c.PSA[:, 0:512], scalar1=1.0 / D, scalar2=None,
            op0=ALU.mult), waits=[("mm", statm[tt * DCH + DCH - 1])])
        q.op("vector", lambda v: v.tensor_scalar(
            out=VAR, in0=c.PSA[:, 512:1024], scalar1=1.0 / D, scalar2=None,
            op0=ALU.mult))
        q.op("vector", lambda v: v.tensor_tensor(out=DD, in0=MEANB, in1=MEANB,
                                                 op=ALU.mult))
        consm[tt] = q.op("vector", lambda v: v.tensor_tensor(
            out=VAR, in0=VAR, in1=DD, op=ALU.subtract))
        sq = q.op("scalar", lambda s: s.activation(out=SD, in_=VAR, func=AF.Sqrt,
                                                   bias=c.EPS[:]),
                  waits=[("ev", consm[tt])])
        q.op("vector", lambda v: v.reciprocal(out=RINV, in_=SD), waits=[("sc", sq)])
        for ch in range(DCH):
            q.op("vector", lambda v, ch=ch, tt=tt: v.tensor_tensor(
                out=DD, in0=at(c, ch, tt), in1=MEANB, op=ALU.subtract))
            q.op("vector", lambda v, ch=ch, tt=tt: v.tensor_tensor(
                out=at(c, ch, tt), in0=DD, in1=RINV, op=ALU.mult))
            if write_xb:
                q.op("vector", lambda v, ch=ch, tt=tt: v.tensor_tensor(
                    out=xt(c, ch, tt), in0=DD, in1=RINV, op=ALU.mult))
    q.play("ln")



def ph_attn_pre(c, L, q=None):
    standalone = q is None
    if standalone:
        q = Sch(c)
    kwh = din(c, f"l{L}_kw", [D, D])
    vwh = din(c, f"l{L}_vw", [D, D])
    bkh = din(c, f"l{L}_bk", [128, 8])
    bqh = din(c, f"l{L}_bq", [128, 8])
    bvh = din(c, f"l{L}_bv_row", [1, 1024])
    kin = dint(c, "kin", [D, TOK])
    vin = dint(c, "vin", [TOK, D])
    kout = dint(c, "kout", [2 * D, TOK])
    vout = dint(c, "vout", [SEQ, D])
    BVB = stg(c, 2048, 1024)
    ROWS = c.STG[0:1, 3072:4096]

    def vslot(tc):
        return c.SCR[:, 4096 + (tc % 2) * 1024: 4096 + (tc % 2) * 1024 + 1024]

    def kslot(oc):
        return c.SCR[:, (oc % 2) * TOK: (oc % 2) * TOK + TOK]

    q.op("sync", lambda s: s.dma_start(out=c.BK[:], in_=bkh.ap()), dma="ld0")
    q.op("sync", lambda s: s.dma_start(out=c.BQR[:], in_=bqh.ap()), dma="ld0")
    mrow = q.op("sync", lambda s: s.dma_start(out=ROWS[0:1, 0:1024], in_=bvh.ap()),
                dma="ld0")
    c.bias_mark = mrow
    kwm = q.op("gpsimd", lambda g: g.dma_start(
        out=c.ARN[:, 0:8192].rearrange("p (a t) -> p a t", a=DCH),
        in_=kwh.rearrange("(a p) t -> p a t", p=128)), dma="sw1")
    vwm = q.op("gpsimd", lambda g: g.dma_start(
        out=c.ARN[:, 8192:16384].rearrange("p (a t) -> p a t", a=DCH),
        in_=vwh.rearrange("(a p) t -> p a t", p=128)), dma="sw2")
    q.op("tensor", lambda t: t.matmul(c.PSC[:, 0:512], c.ONES_F[0:1, :],
                                      ROWS[0:1, 0:512], start=True, stop=True),
         waits=[("ld0", mrow)])
    mb = q.op("tensor", lambda t: t.matmul(c.PSC[:, 512:1024], c.ONES_F[0:1, :],
                                           ROWS[0:1, 512:1024], start=True,
                                           stop=True), inc="mm")
    bvbm = q.op("scalar", lambda s: s.copy(out=BVB, in_=c.PSC[:, 0:1024]),
                waits=[("mm", mb)])
    q.op("scalar", lambda s: s.mul(out=c.BQ8[:], in_=c.BQR[:], mul=0.125))
    vevm, vstm = {}, {}
    for tc in range(16):
        pl = c.PSD[:] if tc % 2 == 0 else c.PSC[:, 0:512]
        pr = c.PSE[:] if tc % 2 == 0 else c.PSC[:, 512:1024]
        w = [("sw2", vwm)]
        if tc == 1:
            w.append(("sc", bvbm))
        if tc >= 2:
            w.append(("ev", vevm[tc - 2]))
        mlast = None
        for k in range(DCH):
            xs = c.XB[:, k * TOK + tc * 128: k * TOK + tc * 128 + 128]
            q.op("tensor", lambda t, xs=xs, k=k, pl=pl: t.matmul(
                pl, xs, arn(c, 8192 + k * 1024, 512),
                start=(k == 0), stop=(k == DCH - 1)), waits=(w if k == 0 else ()))
            mlast = q.op("tensor", lambda t, xs=xs, k=k, pr=pr: t.matmul(
                pr, xs, arn(c, 8192 + k * 1024 + 512, 512),
                start=(k == 0), stop=(k == DCH - 1)),
                inc=("mm" if k == DCH - 1 else None))
        w2 = [("mm", mlast), ("sc", bvbm)]
        if tc >= 2:
            w2.append((f"st{tc % 2}", vstm[tc - 2]))
        q.op("vector", lambda v, tc=tc, pl=pl: v.tensor_tensor(
            out=vslot(tc)[:, 0:512], in0=pl, in1=BVB[:, 0:512], op=ALU.add),
            waits=w2)
        vevm[tc] = q.op("vector", lambda v, tc=tc, pr=pr: v.tensor_tensor(
            out=vslot(tc)[:, 512:1024], in0=pr, in1=BVB[:, 512:1024], op=ALU.add))
        vstm[tc] = q.op("sync", lambda s, tc=tc: s.dma_start(
            out=vin.ap()[tc * 128:(tc + 1) * 128, :], in_=vslot(tc)),
            waits=[("ev", vevm[tc])], dma=f"st{tc % 2}")
    kevm, kstm = {}, {}
    for oc in range(DCH):
        for tt in range(TTN):
            k = oc * TTN + tt
            ps = c.PSA[:, 0:512] if k % 2 == 0 else c.PSB[:, 0:512]
            w = [("sw1", kwm)]
            if k >= 2:
                w.append(("ev", kevm[k - 2]))
            mlast = None
            for kk in range(DCH):
                mlast = q.op("tensor", lambda t, kk=kk, oc=oc, tt=tt, ps=ps:
                             t.matmul(ps, arn(c, kk * 1024 + oc * 128, 128),
                                      xt(c, kk, tt),
                                      start=(kk == 0), stop=(kk == DCH - 1)),
                             waits=(w if kk == 0 else ()),
                             inc=("mm" if kk == DCH - 1 else None))
            w2 = [("mm", mlast), ("ld0", mrow)]
            if oc >= 2 and tt == 0:
                w2.append((f"st{oc % 2}", kstm[oc - 2]))
            kevm[k] = q.op("vector", lambda v, oc=oc, tt=tt, ps=ps:
                           v.tensor_scalar(out=kslot(oc)[:, tt * NT: tt * NT + NT],
                                           in0=ps, scalar1=c.BK[:, oc:oc + 1],
                                           scalar2=None, op0=ALU.add), waits=w2)
        kstm[oc] = q.op("sync", lambda s, oc=oc: s.dma_start(
            out=kin.ap()[oc * 128:(oc + 1) * 128, :], in_=kslot(oc)),
            waits=[("ev", kevm[oc * TTN + TTN - 1])], dma=f"st{oc % 2}")
    ccm = q.op("gpsimd", lambda g: g.collective_compute(
        "AllGather", ALU.bypass, replica_groups=GROUPS,
        ins=[kin.ap().opt()], outs=[kout.ap().opt()]),
        waits=[("st0", kstm[DCH - 2]), ("st1", kstm[DCH - 1]),
               ("st0", vstm[14]), ("st1", vstm[15])], inc="cc")
    q.op("gpsimd", lambda g: g.collective_compute(
        "AllGather", ALU.bypass, replica_groups=GROUPS,
        ins=[vin.ap().opt()], outs=[vout.ap().opt()]),
        waits=[("cc", ccm)], inc="cc")
    c.pre_kev = kevm[DCH * TTN - 1]
    c.pre_vev = vevm[15]
    if standalone:
        q.play(f"attnpre{L}")


def ph_attn_main(c, L):
    q = Sch(c)
    ph_attn_pre(c, L, q=q)
    causal = (L == 1)
    qwh = din(c, f"l{L}_qw", [D, D])
    kout = dint(c, "kout", [2 * D, TOK])
    vout = dint(c, "vout", [SEQ, D])
    vre = vout.rearrange("(a p) d -> p a d", p=128)
    LT = arn(c, 22144, 1024)
    LBC = stg(c, 2048, 1024)
    QW = 20608
    TMPB = arn(c, 21632, 512)
    ccv = c.sem["cc"].v

    def vb(buf):
        return c.ARN[:, 4096 + buf * 4160: 4096 + (buf + 1) * 4160].rearrange(
            "p (a g) -> p a g", g=130)

    def khp(buf, kt):
        return c.ARN[:, kt * 128: kt * 128 + 128]

    def qhp(buf, qt):
        return c.ARN[:, 12416 + buf * TOK + qt * NT: 12416 + buf * TOK + qt * NT + NT]

    def ptbase(kt):
        sl = kt % 5
        return 16512 + sl * 1024 if sl < 4 else 23168

    def pt(kt, head):
        o = ptbase(kt) + head * 512
        return c.ARN[:, o: o + 512]

    q.op("gpsimd", lambda g: g.memset(
        c.ARN[:, 4096:12416].rearrange("p (a g) -> p a g", g=130)[:, :, 64:65], 1.0))
    vones = q.op("gpsimd", lambda g: g.memset(
        c.ARN[:, 4096:12416].rearrange("p (a g) -> p a g", g=130)[:, :, 129:130], 1.0))

    qcast, kvld = {}, {}
    qpm, qem, sm, xm, mkm, avm, rm, bcm, lbcm, oam, shm, obm = ({} for _ in range(12))
    for hp in range(NHP):
        buf = hp % 2
        w = [] if hp == 0 else [("mm", qpm[(hp - 1, TTN - 1)])]
        qcast[hp] = q.op("gpsimd", lambda g, hp=hp: g.dma_start(
            out=arn(c, QW, 1024).rearrange("p (a t) -> p a t", a=DCH),
            in_=qwh.rearrange("(a p) t -> p a t", p=128)
            [:, :, hp * 128:(hp + 1) * 128]), waits=w, dma="sw3")
        w = [("mm", shm[(hp - 2, TTN - 1)])] if hp >= 2 else []
        if hp == 0:
            w.append(("cc", ccv))
        if hp >= 1:
            w.append(("mm", sm[(hp - 1, TTN - 1, NKT - 1)]))
        kv = f"ld{1 + hp % 2}"
        q.op("sync", lambda s, hp=hp, buf=buf: s.dma_start(
            out=c.ARN[:, 0:TOK],
            in_=kout.ap()[hp * 128:(hp + 1) * 128, :]), waits=w, dma=kv)
        q.op("sync", lambda s, hp=hp, buf=buf: s.dma_start(
            out=c.ARN[:, TOK:SEQ],
            in_=kout.ap()[D + hp * 128: D + hp * 128 + 128, :]), dma=kv)
        q.op("sync", lambda s, hp=hp, buf=buf: s.dma_start(
            out=vb(buf)[:, :, 0:64], in_=vre[:, :, hp * 128: hp * 128 + 64]),
            dma=kv)
        kvld[hp] = q.op("sync", lambda s, hp=hp, buf=buf: s.dma_start(
            out=vb(buf)[:, :, 65:129],
            in_=vre[:, :, hp * 128 + 64: hp * 128 + 128]), dma=kv)
        for tt in range(TTN):
            ps = c.PSA[:, 0:512] if tt % 2 == 0 else c.PSB[:, 0:512]
            w = [("sw3", qcast[hp])]
            if hp == 0 and tt == 0:
                w.append(("ld0", c.bias_mark))
                w.append(("ev", c.pre_kev))
            if hp >= 1 and tt == 0:
                w.append(("sc", xm[(hp - 1, TTN - 1, NKT - 1)]))
            if tt >= 2:
                w.append(("ev", qem[(hp, tt - 2)]))
            for kk in range(DCH):
                qpm[(hp, tt)] = q.op("tensor", lambda t, kk=kk, tt=tt, ps=ps:
                                     t.matmul(ps, arn(c, QW + kk * 128, 128),
                                              xt(c, kk, tt), start=(kk == 0),
                                              stop=(kk == DCH - 1)),
                                     waits=(w if kk == 0 else ()),
                                     inc=("mm" if kk == DCH - 1 else None))
            qem[(hp, tt)] = q.op("vector", lambda v, tt=tt, ps=ps, buf=buf, hp=hp:
                                 v.tensor_scalar(out=qhp(buf, tt), in0=ps,
                                                 scalar1=c.BQR[:, hp:hp + 1],
                                                 scalar2=0.125, op0=ALU.add,
                                                 op1=ALU.mult),
                                 waits=[("mm", qpm[(hp, tt)])])
        for qt in range(TTN):
            LAG = 4
            for step in range(NKT + LAG):
                if step < NKT:
                    kt = step
                    sps = c.PSA if kt % 2 == 0 else c.PSB
                    w = [(f"ld{1 + hp % 2}", kvld[hp])]
                    if (qt, kt) == (0, 0):
                        w.append(("ev", qem[(hp, TTN - 1)]))
                        w.append(("gp", vones))
                    if kt >= 2:
                        w.append(("sc", xm[(hp, qt, kt - 2)]))
                    elif qt >= 1:
                        w.append(("sc", xm[(hp, qt - 1, NKT - 2 + kt)]))
                    q.op("tensor", lambda t, kt=kt, qt=qt, sps=sps, buf=buf:
                         t.matmul(sps[:, 0:512], khp(buf, kt)[0:64, :],
                                  qhp(buf, qt)[0:64, :],
                                  start=True, stop=True, tile_position=(0, 0)),
                         waits=w)
                    sm[(hp, qt, kt)] = q.op(
                        "tensor", lambda t, kt=kt, qt=qt, sps=sps, buf=buf:
                        t.matmul(sps[:, 512:1024], khp(buf, kt)[64:128, :],
                                 qhp(buf, qt)[64:128, :], start=True, stop=True,
                                 tile_position=(64, 0)), inc="mm")
                    w = [("mm", sm[(hp, qt, kt)])]
                    if kt >= 5:
                        w.append(("mm", avm[(hp, qt, kt - 5)]))
                    elif qt >= 1:
                        w.append(("mm", avm[(hp, qt - 1, NKT - 5 + kt)]))
                    elif hp >= 1:
                        w.append(("mm", avm[(hp - 1, TTN - 1, NKT - 5 + kt)]))
                    xm[(hp, qt, kt)] = q.op("scalar", lambda s, kt=kt, sps=sps:
                                            s.activation(
                        out=c.ARN[:, ptbase(kt): ptbase(kt) + 1024],
                        in_=sps[:, 0:1024], func=AF.Exp), waits=w, chain=False)
                    if causal:
                        tc_ = c.THR[:, qt * NKT + kt: qt * NKT + kt + 1]
                        pb_ = c.ARN[:, ptbase(kt): ptbase(kt) + 1024]
                        mkm[(hp, qt, kt)] = q.op(
                            "vector", lambda v, pb_=pb_, tc_=tc_:
                            v.scalar_tensor_tensor(
                                out=pb_, in0=c.IOTA2[:], scalar=tc_,
                                in1=pb_, op0=ALU.is_ge, op1=ALU.mult),
                            waits=[("sc", xm[(hp, qt, kt)])], chain=False)
                if step >= LAG:
                    kt = step - LAG
                    if causal:
                        wav = [("ev", mkm[(hp, qt, kt)])]
                    else:
                        wav = [("sc", xm[(hp, qt, kt)])]
                    if qt >= 1 and kt == 0:
                        wav.append(("sc", obm[(hp, qt - 1)]))
                    elif hp >= 1 and qt == 0 and kt == 0:
                        wav.append(("sc", obm[(hp - 1, TTN - 1)]))
                    elif (hp, qt, kt) == (0, 0, 0):
                        wav.append(("ev", c.pre_vev))
                    q.op("tensor", lambda t, kt=kt, buf=buf: t.matmul(
                        c.PSD[0:65, :], vb(buf)[:, kt, 0:65], pt(kt, 0),
                        start=(kt == 0), stop=(kt == NKT - 1)), waits=wav)
                    avm[(hp, qt, kt)] = q.op("tensor", lambda t, kt=kt, buf=buf:
                                             t.matmul(
                        c.PSE[0:65, :], vb(buf)[:, kt, 65:130], pt(kt, 1),
                        start=(kt == 0), stop=(kt == NKT - 1)), inc="mm")
            def _recA(v):
                with c.nc.allow_low_precision(
                        reason="1/l broadcast fed to a bf16 matmul operand"):
                    return v.reciprocal(out=LT[64:65, 0:512], in_=c.PSD[64:65, :])

            def _recB(v):
                with c.nc.allow_low_precision(
                        reason="1/l broadcast fed to a bf16 matmul operand"):
                    return v.reciprocal(out=LT[64:65, 512:1024], in_=c.PSE[64:65, :])

            q.op("vector", _recA, waits=[("mm", avm[(hp, qt, NKT - 1)])])
            rm[(hp, qt)] = q.op("vector", _recB)
            w = [("ev", rm[(hp, qt)])]
            if (hp, qt) != (0, 0):
                pq = (hp, qt - 1) if qt >= 1 else (hp - 1, TTN - 1)
                w.append(("sc", lbcm[pq]))
            q.op("tensor", lambda t: t.matmul(
                c.PSC[:, 0:512], c.ONES_B[64:65, :], LT[64:65, 0:512],
                start=True, stop=True), waits=w)
            bcm[(hp, qt)] = q.op("tensor", lambda t: t.matmul(
                c.PSC[:, 512:1024], c.ONES_B[64:65, :], LT[64:65, 512:1024],
                start=True, stop=True), inc="mm")
            w = [("mm", bcm[(hp, qt)])]
            if (hp, qt) != (0, 0):
                pq = (hp, qt - 1) if qt >= 1 else (hp - 1, TTN - 1)
                w.append(("ev", oam[pq]))
            lbcm[(hp, qt)] = q.op("scalar", lambda s: s.copy(
                out=LBC, in_=c.PSC[:, 0:1024]), waits=w)
            och = c.SCR[:, hp * TOK + qt * NT: hp * TOK + qt * NT + NT]
            q.op("vector", lambda v, och=och: v.tensor_tensor(
                out=och[0:64, :], in0=c.PSD[0:64, :], in1=LBC[0:64, 0:512],
                op=ALU.mult), waits=[("sc", lbcm[(hp, qt)])])
            oam[(hp, qt)] = q.op("vector", lambda v: v.tensor_tensor(
                out=TMPB[0:64, :], in0=c.PSE[0:64, :], in1=LBC[0:64, 512:1024],
                op=ALU.mult))
            shm[(hp, qt)] = q.op("tensor", lambda t: t.matmul(
                c.PSD[64:128, :], c.IDENTB[0:64, 0:64], TMPB[0:64, :],
                start=True, stop=True, tile_position=(0, 64)),
                waits=[("ev", oam[(hp, qt)])], inc="mm")
            obm[(hp, qt)] = q.op("scalar", lambda s, och=och: s.copy(
                out=och[64:128, :], in_=c.PSD[64:128, :]),
                waits=[("mm", shm[(hp, qt)])])
    q.play(f"attnmain{L}")


def ph_attn_proj(c, L):
    q = Sch(c)
    pwh = din(c, f"l{L}_pw", [D, D])
    pbh = din(c, f"l{L}_pb", [128, 8])
    mpb = q.op("sync", lambda s: s.dma_start(out=c.BP[:], in_=pbh.ap()), dma="ld0")
    pwm = q.op("gpsimd", lambda g: g.dma_start(
        out=c.ARN[:, 0:8192].rearrange("p (a t) -> p a t", a=DCH),
        in_=pwh.rearrange("(a p) t -> p a t", p=128)), dma="sw1")
    ym, evm = {}, {}
    for do in range(DCH):
        for tt in range(TTN):
            k = do * TTN + tt
            ps = c.PSD[:] if k % 2 == 0 else c.PSE[:]
            w = [("sw1", pwm), ("ld0", mpb)]
            if k >= 2:
                w.append(("ev", evm[k - 2]))
            for hp in range(NHP):
                ym[k] = q.op("tensor", lambda t, hp=hp, do=do, tt=tt, ps=ps:
                             t.matmul(ps, arn(c, hp * 1024 + do * 128, 128),
                                      c.SCR[:, hp * TOK + tt * NT:
                                            hp * TOK + tt * NT + NT],
                                      start=(hp == 0), stop=(hp == NHP - 1)),
                             waits=(w if hp == 0 else ()),
                             inc=("mm" if hp == NHP - 1 else None))
            evm[k] = q.op("vector", lambda v, do=do, tt=tt, ps=ps:
                          v.scalar_tensor_tensor(
                out=at(c, do, tt), in0=ps, scalar=c.BP[:, do:do + 1],
                in1=at(c, do, tt), op0=ALU.add, op1=ALU.add),
                waits=[("mm", ym[k])])
    q.play(f"attnproj{L}")



def ph_gate(c, L):
    q = Sch(c)
    wgh = din(c, f"l{L}_wg", [D, 8])
    rowh = din(c, f"l{L}_bg16", [1, 128])
    ZT, WT = stg(c, 2048, 128), stg(c, 2176, 128)
    M8, ZAP = stg(c, 2304, 128), stg(c, 2432, 128)
    MSK, QG, CT = stg(c, 2560, 128), stg(c, 2688, 128), stg(c, 2816, 128)
    Q16, D16, R16, T16 = (stg(c, 2944, 16), stg(c, 2960, 16),
                          stg(c, 2976, 16), stg(c, 2992, 16))
    ROWS = c.STG[0:1, 3072:3200]
    CFM = c.ARN[0:8, 18432:20480]

    ldw = q.op("sync", lambda s: s.dma_start(
        out=c.STG[:, 0:64].rearrange("p (a t) -> p a t", a=DCH, t=8),
        in_=wgh.rearrange("(a p) t -> p a t", p=128)), dma="ld0")
    ldr = q.op("sync", lambda s: s.dma_start(out=ROWS[0:1, :], in_=rowh.ap()),
               dma="ld0")
    gw = q.op("gpsimd", lambda g: g.tensor_copy(out=c.ARN[:, 0:64],
                                                in_=c.STG[:, 0:64]),
              waits=[("ld0", ldr)])
    bb = q.op("tensor", lambda t: t.matmul(c.PSB[:, 0:128], c.ONES_F[0:1, :],
                                           ROWS[0:1, :], start=True, stop=True),
              waits=[("ld0", ldr)], inc="mm")
    zm = None
    for tc in range(16):
        for kk in range(DCH):
            zm = q.op("tensor", lambda t, tc=tc, kk=kk: t.matmul(
                c.PSA[:, tc * 8: tc * 8 + 8],
                c.XB[:, kk * TOK + tc * 128: kk * TOK + tc * 128 + 128],
                c.ARN[:, kk * 8: kk * 8 + 8],
                start=(kk == 0), stop=(kk == DCH - 1)),
                waits=([("gp", gw)] if (tc, kk) == (0, 0) else ()),
                inc=("mm" if (tc, kk) == (15, DCH - 1) else None))
    q.op("vector", lambda v: v.tensor_copy(out=ZT, in_=c.PSA[:, 0:128]),
         waits=[("mm", zm), ("mm", bb)])
    q.op("vector", lambda v: v.tensor_tensor(out=WT, in0=ZT,
                                             in1=c.PSB[:, 0:128], op=ALU.add))
    for tc in range(16):
        q.op("vector", lambda v, tc=tc: v.max(out=M8[:, tc * 8: tc * 8 + 8],
                                              in_=WT[:, tc * 8: tc * 8 + 8]))
    if L == 0:
        q.op("scalar", lambda s: s.activation(
            out=D16.rearrange("p (a o) -> p a o", o=1),
            in_=M8.rearrange("p (a g) -> p a g", g=8)[:, :, 0:1], func=AF.Exp),
            waits=[("ev", c.sem["ev"].v)])
        q.op("scalar", lambda s: s.activation(
            out=R16.rearrange("p (a o) -> p a o", o=1),
            in_=M8.rearrange("p (a g) -> p a g", g=8)[:, :, 1:2], func=AF.Exp))
        sgq = q.op("scalar", lambda s: s.activation(out=QG, in_=WT, func=AF.Exp))
    else:
        sgq = q.op("scalar", lambda s: s.activation(out=QG, in_=ZT,
                                                    func=AF.Sigmoid),
                   waits=[("ev", c.sem["ev"].v)])
    q.op("vector", lambda v: v.memset(
        M8.rearrange("p (a g) -> p a g", g=8)[:, :, 2:8], -2.0e30),
        waits=[("sc", sgq)])
    for tc in range(16):
        q.op("vector", lambda v, tc=tc: v.match_replace(
            out=ZAP[:, tc * 8: tc * 8 + 8], in_to_replace=M8[:, tc * 8: tc * 8 + 8],
            in_values=WT[:, tc * 8: tc * 8 + 8], imm_value=-2.0e30))
    q.op("vector", lambda v: v.tensor_tensor(out=MSK, in0=WT, in1=ZAP,
                                             op=ALU.not_equal))
    ctm = q.op("vector", lambda v: v.tensor_tensor(out=CT, in0=QG, in1=MSK,
                                                   op=ALU.mult))
    if L == 0:
        q.op("vector", lambda v: v.reduce_sum(
            Q16.rearrange("p (a o) -> p a o", o=1),
            QG.rearrange("p (a g) -> p a g", g=8), axis=AX.X))
        q.op("vector", lambda v: v.tensor_tensor(out=D16, in0=D16, in1=R16,
                                                 op=ALU.add))
        q.op("vector", lambda v: v.tensor_scalar(out=T16, in0=Q16, scalar1=1e-8,
                                                 scalar2=None, op0=ALU.mult))
        q.op("vector", lambda v: v.tensor_tensor(out=D16, in0=D16, in1=T16,
                                                 op=ALU.add))
        q.op("vector", lambda v: v.reciprocal(out=R16, in_=D16))
        for tc in range(16):
            ctm = q.op("vector", lambda v, tc=tc: v.tensor_scalar(
                out=CT[:, tc * 8:tc * 8 + 8], in0=CT[:, tc * 8:tc * 8 + 8],
                scalar1=R16[:, tc:tc + 1], scalar2=None, op0=ALU.mult))
    trm = {}
    for tc in range(16):
        ps = c.PSA if tc < 8 else c.PSB
        trm[tc] = q.op("tensor", lambda t, tc=tc, ps=ps: t.transpose(
            ps[0:8, (tc % 8) * 128: (tc % 8) * 128 + 128],
            CT[:, tc * 8: tc * 8 + 8], c.IDENT[:]),
            waits=([("ev", ctm)] if tc == 0 else ()),
            inc=("mm" if tc in (7, 15) else None))
    q.op("scalar", lambda s: s.copy(out=CFM[0:8, 0:1024], in_=c.PSA[0:8, :]),
         waits=[("mm", trm[7])])
    q.op("scalar", lambda s: s.copy(out=CFM[0:8, 1024:2048], in_=c.PSB[0:8, :]),
         waits=[("mm", trm[15])])
    q.play(f"gate{L}")


def ph_b2c(c):
    q = Sch(c)
    CFM = c.ARN[0:8, 18432:20480]
    evm = {}
    for do in range(DCH):
        for tt in range(TTN):
            k = do * TTN + tt
            ps = c.PSD[:] if k % 2 == 0 else c.PSE[:]
            w = [("ev", evm[k - 2])] if k >= 2 else []
            m = q.op("tensor", lambda t, do=do, tt=tt, ps=ps: t.matmul(
                ps, c.B2BF[0:8, do * 128:(do + 1) * 128],
                CFM[0:8, tt * NT: tt * NT + NT], start=True, stop=True),
                waits=w, inc="mm")
            evm[k] = q.op("vector", lambda v, do=do, tt=tt, ps=ps:
                          v.tensor_tensor(out=at(c, do, tt), in0=at(c, do, tt),
                                          in1=ps, op=ALU.add),
                          waits=[("mm", m)])
    q.play("b2c")



def ph_experts(c, L):
    q = Sch(c)
    CFM = c.ARN[0:8, 18432:20480]
    CBC = arn(c, 16384, 2048)

    if L == 0:
        NE, NFS, FCS, FSW = 8, 4, 8, 1024
        w1h = din(c, "l0_w1", [8, D, 4096])
        w2h = din(c, "l0_w2", [8, 4096, D])
        uwh = sgh = suh = sdh = None
    else:
        NE, NFS, FCS, FSW = 9, 4, 4, 512
        w1h = din(c, "l1_w1", [8, D, 2048])
        uwh = din(c, "l1_uw", [8, D, 2048])
        w2h = din(c, "l1_w2", [8, 2048, D])
        sgh = din(c, "l1_sg", [D, 2048])
        suh = din(c, "l1_su", [D, 2048])
        sdh = din(c, "l1_sd", [2048, D])

    def w1slice(e, fs):
        src = w1h.ap()[e] if (L == 0 or e < 8) else sgh.ap()
        return src.rearrange("(a p) f -> p a f", p=128)[:, :, fs * FSW:(fs + 1) * FSW]

    def wuslice(e, fs):
        src = uwh.ap()[e] if e < 8 else suh.ap()
        return src.rearrange("(a p) f -> p a f", p=128)[:, :, fs * FSW:(fs + 1) * FSW]

    def w2slice(e, fs):
        src = w2h.ap()[e] if (L == 0 or e < 8) else sdh.ap()
        return src[fs * FSW:(fs + 1) * FSW, :].rearrange("(a p) f -> p a f", p=128)

    def hsc(fc, tt):
        return c.SCR[:, fc * 2048 + tt * 512: fc * 2048 + tt * 512 + 512]

    p1last = {}
    p2last = {}
    castm = {}
    cbcm = {}
    hsm_all = {}
    hvm_all = {}
    yem_all = {}

    LSEM = {"w1": "sw1", "wu": "sw3", "w2": "sw2"}

    def emit_loads(sl, e, fs):
        if L == 0:
            srcs = [("w1", w1slice(e, fs), 0), ("w2", w2slice(e, fs), 8192)]
        else:
            srcs = [("w1", w1slice(e, fs), 0), ("wu", wuslice(e, fs), 4096),
                    ("w2", w2slice(e, fs), 8192)]
        for kind, src, dsto in srcs:
            ncnk, width = src.shape[1], src.shape[2]
            w = []
            if sl >= 1:
                if kind in ("w1", "wu"):
                    w.append(("mm", p1last[sl - 1]))
                else:
                    w.append(("mm", p2last[sl - 1]))
            m = q.op("gpsimd", lambda g, src=src, dsto=dsto, ncnk=ncnk,
                     width=width: g.dma_start(
                out=arn(c, dsto, ncnk * width).rearrange(
                    "p (a t) -> p a t", a=ncnk), in_=src),
                waits=w, dma=LSEM[kind])
            castm[(sl, kind)] = (LSEM[kind], m)

    for e in range(NE):
        if e < 8:
            sl0 = e * NFS
            for pair in range(2):
                ms = None
                for tt in range(2 * pair, 2 * pair + 2):
                    w = []
                    if L == 1 and sl0 >= 1 and tt == 2 * pair:
                        w.append(("ev", hvm_all[(sl0 - 1, FCS * TTN - 1)]))
                    if pair == 1 and tt == 2:
                        w.append(("sc", cbcm[e]))
                    ms = q.op("tensor", lambda t, tt=tt, e=e: t.matmul(
                        c.PSC[:, (tt % 2) * 512: (tt % 2) * 512 + 512],
                        c.SEL[0:8, e * 128:(e + 1) * 128],
                        CFM[0:8, tt * NT: tt * NT + NT], start=True, stop=True),
                        waits=w, inc=("mm" if tt % 2 == 1 else None))
                w = [("mm", ms)]
                if e >= 1 and pair == 0:
                    w.append(yem_all[(sl0 - 1, DCH * TTN - 1)])
                q.op("scalar", lambda s, pair=pair: s.copy(
                    out=CBC[:, pair * 1024: pair * 1024 + 512],
                    in_=c.PSC[:, 0:512]), waits=w)
                cbcm[e] = q.op("scalar", lambda s, pair=pair: s.copy(
                    out=CBC[:, pair * 1024 + 512: pair * 1024 + 1024],
                    in_=c.PSC[:, 512:1024]))

        for fs in range(NFS):
            sl = e * NFS + fs
            emit_loads(sl, e, fs)
            hm, hsm, hvm, ym, yem = {}, {}, {}, {}, {}
            for fc in range(FCS):
                for tt in range(TTN):
                    k = fc * TTN + tt
                    pg = c.PSD[:] if k % 2 == 0 else c.PSE[:]
                    pu = c.PSC[:, 0:512] if k % 2 == 0 else c.PSC[:, 512:1024]
                    w = []
                    if k == 0:
                        w.append(castm[(sl, "w1")])
                        if L == 1:
                            w.append(castm[(sl, "wu")])
                            if e < 8:
                                w.append(("sc", cbcm[e]))
                    if k >= 2:
                        w.append((("sc", hsm[k - 2]) if L == 0
                                  else ("ev", hvm[k - 2])))
                    elif sl >= 1:
                        kk2 = k - 2 + FCS * TTN
                        w.append((("sc", hsm_all[(sl - 1, kk2)]) if L == 0
                                  else ("ev", hvm_all[(sl - 1, kk2)])))
                        w.append(yem_all[(sl - 1, 28 + k)])
                    for kk in range(DCH):
                        hm[k] = q.op("tensor", lambda t, kk=kk, fc=fc, tt=tt,
                                     pg=pg: t.matmul(
                            pg, arn(c, kk * FSW + fc * 128, 128), xt(c, kk, tt),
                            start=(kk == 0), stop=(kk == DCH - 1)),
                            waits=(w if kk == 0 else ()),
                            inc=("mm" if (L == 0 and kk == DCH - 1) else None))
                        if L == 1:
                            hm[k] = q.op("tensor", lambda t, kk=kk, fc=fc, tt=tt,
                                         pu=pu: t.matmul(
                                pu, arn(c, 4096 + kk * FSW + fc * 128, 128),
                                xt(c, kk, tt),
                                start=(kk == 0), stop=(kk == DCH - 1)),
                                inc=("mm" if kk == DCH - 1 else None))
                    if L == 0:
                        bias = c.B1[:, e * 32 + fs * 8 + fc:
                                    e * 32 + fs * 8 + fc + 1]
                        hsm[k] = q.op("scalar", lambda s, fc=fc, tt=tt, pg=pg,
                                      bias=bias: s.activation(
                            out=hsc(fc, tt), in_=pg, func=AF.Gelu_apprx_tanh,
                            bias=bias), waits=[("mm", hm[k])])
                    else:
                        w2_ = [("mm", hm[k])]
                        if k >= 2:
                            w2_.append(("ev", hvm[k - 2]))
                        hsm[k] = q.op("scalar", lambda s, k=k, pg=pg:
                                      s.activation(
                            out=stg(c, 2560 + (k % 2) * 512, 512), in_=pg,
                            func=AF.Silu), waits=w2_)
                        hvm[k] = q.op("vector", lambda v, fc=fc, tt=tt, k=k,
                                      pu=pu: v.tensor_tensor(
                            out=hsc(fc, tt), in0=stg(c, 2560 + (k % 2) * 512, 512),
                            in1=pu, op=ALU.mult), waits=[("sc", hsm[k])])
                    hsm_all[(sl, k)] = hsm[k]
                    if L == 1:
                        hvm_all[(sl, k)] = hvm[k]
            YB = [c.PSA[:, 0:512], c.PSA[:, 512:1024], c.PSB[:, 0:512],
                  c.PSB[:, 512:1024], c.PSD[:], c.PSE[:]]
            PREVK = [30, 31, 26, 27, 28, 29]
            for do in range(DCH):
                for tt in range(TTN):
                    k = do * TTN + tt
                    ps = YB[k % 6]
                    w = []
                    if k == 0:
                        w.append(castm[(sl, "w2")])
                    if k >= 6:
                        w.append(yem[k - 6])
                    elif sl >= 1:
                        w.append(yem_all[(sl - 1, PREVK[k])])
                    if k in (4, 5):
                        kk2 = FCS * TTN - 2 + (k - 4)
                        w.append((("sc", hsm[kk2]) if L == 0
                                  else ("ev", hvm[kk2])))
                    w.append((("sc", hsm[(FCS - 1) * TTN + tt]) if L == 0
                              else ("ev", hvm[(FCS - 1) * TTN + tt])))
                    for fc in range(FCS):
                        ym[k] = q.op("tensor", lambda t, fc=fc, do=do, tt=tt,
                                     ps=ps: t.matmul(
                            ps, arn(c, 8192 + fc * 1024 + do * 128, 128),
                            hsc(fc, tt), start=(fc == 0), stop=(fc == FCS - 1)),
                            waits=(w if fc == 0 else ()),
                            inc=("mm" if fc == FCS - 1 else None))
                    prev_tile = yem_all.get((sl - 1, k))
                    if e < 8:
                        ww = [("mm", ym[k])]
                        if k == 0:
                            ww.append(("sc", cbcm[e]))
                        ysl = 2048 + (k % 2) * 1536
                        if k >= 2:
                            ww.append(yem[k - 2])
                        mq = q.op("vector", lambda v, tt=tt, ps=ps, ysl=ysl:
                                  v.tensor_tensor(
                            out=stg(c, ysl, 512), in0=ps,
                            in1=CBC[:, tt * NT: tt * NT + NT], op=ALU.mult),
                            waits=ww)
                        gm = q.op("vector", lambda v, do=do, tt=tt, ysl=ysl:
                                  v.tensor_tensor(
                            out=at(c, do, tt), in0=at(c, do, tt),
                            in1=stg(c, ysl, 512), op=ALU.add),
                            waits=[prev_tile])
                        yem[k] = ("ev", gm)
                    else:
                        vm = q.op("vector", lambda v, do=do, tt=tt, ps=ps:
                                  v.tensor_tensor(
                            out=at(c, do, tt), in0=at(c, do, tt), in1=ps,
                            op=ALU.add), waits=[("mm", ym[k]), prev_tile])
                        yem[k] = ("ev", vm)
                    yem_all[(sl, k)] = yem[k]
            p1last[sl] = hm[FCS * TTN - 1]
            p2last[sl] = ym[DCH * TTN - 1]
    q.play(f"experts{L}")




PH = {
    "init": ph_init,
    "loadx": ph_load_x,
    "ap0": lambda c: ph_attn_pre(c, 0),
    "am0": lambda c: ph_attn_main(c, 0),
    "aj0": lambda c: ph_attn_proj(c, 0),
    "ap1": lambda c: ph_attn_pre(c, 1),
    "am1": lambda c: ph_attn_main(c, 1),
    "aj1": lambda c: ph_attn_proj(c, 1),
    "ln": ph_ln,
    "lnf": lambda c: ph_ln(c, write_xb=False),
    "g0": lambda c: ph_gate(c, 0),
    "g1": lambda c: ph_gate(c, 1),
    "b2c": ph_b2c,
    "ex0": lambda c: ph_experts(c, 0),
    "ex1": lambda c: ph_experts(c, 1),
    "store": ph_store_stream,
}

FULL = ["init", "loadx", "am0", "aj0", "ln", "g0", "b2c", "ex0", "ln",
        "am1", "aj1", "ln", "g1", "ex1", "lnf", "store"]


def build_program(phases):
    nc = bass.Bass("TRN2", target_bir_lowering=False)
    c = make_ctx(nc)
    c.sem = {}
    with contextlib.ExitStack() as es:
        for name in ["ld0", "ld1", "ld2", "ld3", "st0", "st1", "sw0", "sw1", "sw2", "sw3", "mm", "sc", "ev", "gp", "cc", "sf"]:
            c.sem[name] = Sem(es.enter_context(nc.semaphore(name)))
        for p in phases:
            PH[p](c)
    return nc, c


def make_inmaps(c, inputs, hp=None, streams=None):
    if hp is None:
        hp = host_prep(inputs)
    maps = []
    for core in range(8):
        b, half = core // 2, core % 2
        m = {}
        for n in c.input_names:
            if n == "x_fm_slice":
                m[n] = np.ascontiguousarray(
                    hp["x_fm"][b][:, half * TOK:(half + 1) * TOK])
            elif n == "thr":
                m[n] = core_thr(core)
            elif n == "stream_in":
                m[n] = streams[core]
            else:
                m[n] = hp[n]
        maps.append(m)
    return maps


def kernel(**inputs):
    hp = host_prep(inputs)
    nc, c = build_program(FULL)
    maps = make_inmaps(c, inputs, hp)
    res = run_bass_kernel_spmd(nc, maps, core_ids=list(range(8)))
    out = np.zeros((4, SEQ, D), np.float32)
    for core in range(8):
        b, half = core // 2, core % 2
        shard = np.asarray(res.results[core]["stream_out"])
        out[b, half * TOK:(half + 1) * TOK, :] = shard.T
    return out


# version 14
# speedup vs baseline: 1.0970x; 1.0970x over previous
import contextlib
import numpy as np
import ml_dtypes
import concourse.bass as bass
import concourse.mybir as mybir
from concourse.bass_utils import run_bass_kernel_spmd

F32 = mybir.dt.float32
BF16 = mybir.dt.bfloat16
AF = mybir.ActivationFunctionType
ALU = mybir.AluOpType
AX = mybir.AxisListType
BF = ml_dtypes.bfloat16

TOK = 2048
NT = 512
TTN = 4
D = 1024
DCH = 8
SEQ = 4096
NKT = 32
NHP = 8
LN_EPS = 1e-5
GROUPS = [[0, 1], [2, 3], [4, 5], [6, 7]]
ARNW = 24832

ENGS = ("sync", "tensor", "scalar", "vector", "gpsimd")
SEMOF = {"tensor": "mm", "scalar": "sc", "vector": "ev", "gpsimd": "gp"}
CHAIN = {"scalar", "vector", "gpsimd"}


class Sem:
    def __init__(self, h):
        self.h = h
        self.v = 0


class Ctx:
    pass


class Sch:

    def __init__(self, c):
        self.c = c
        self.q = {e: [] for e in ENGS}
        self.last = {e: None for e in ENGS}

    def op(self, eng, fn, waits=(), inc=None, chain=True, dma=None):
        c = self.c
        w = [x for x in waits if x is not None and x[1] is not None]
        incs = []
        ret = None
        if eng == "gpsimd" and dma is not None:
            c.sem[dma].v += 16
            ret = c.sem[dma].v
            incs = [(dma, 16)]
        elif eng in CHAIN:
            if chain and self.last[eng] is not None:
                w.append(self.last[eng])
            sn = inc if inc is not None else SEMOF[eng]
            c.sem[sn].v += 1
            ret = c.sem[sn].v
            incs = [(sn, 1)]
            self.last[eng] = (sn, ret)
        elif eng == "tensor":
            if inc is not None:
                c.sem[inc].v += 1
                ret = c.sem[inc].v
                incs = [(inc, 1)]
        elif eng == "sync":
            if dma is not None:
                c.sem[dma].v += 16
                ret = c.sem[dma].v
                incs = [(dma, 16)]
        self.q[eng].append((w, fn, incs))
        return ret

    _n = [0]

    def play(self, name):
        c = self.c
        Sch._n[0] += 1
        with c.nc.Block(f"{name}_{Sch._n[0]}") as blk:
            for eng in ENGS:
                if not self.q[eng]:
                    continue

                def runner(e, lst=tuple(self.q[eng])):
                    seen = {}
                    for w, fn, incs in lst:
                        for sn, val in w:
                            if seen.get(sn, -1) >= val:
                                continue
                            e.wait_ge(c.sem[sn].h, val)
                            seen[sn] = val
                        ins = fn(e)
                        for sn, n in incs:
                            ins.then_inc(c.sem[sn].h, n)

                getattr(blk, eng)(runner)
        self.q = {e: [] for e in ENGS}
        self.last = {e: None for e in ENGS}



def host_prep(inp):
    h = {}
    f32 = np.float32

    def c(a):
        return np.ascontiguousarray(np.asarray(a, f32))

    h["x_fm"] = c(np.transpose(np.asarray(inp["L0_attn_x"], f32), (0, 2, 1)))
    qkv = np.asarray(inp["L0_attn_qkv_w"], f32)
    h["l0_qw"] = c(qkv[:, 0:1024])
    h["l0_kw"] = c(qkv[:, 1024:2048])
    h["l0_vw"] = c(qkv[:, 2048:3072])
    qb = np.asarray(inp["L0_attn_qkv_b"], f32)
    h["l0_bq"] = c(qb[0:1024].reshape(8, 128).T)
    h["l0_bk"] = c(qb[1024:2048].reshape(8, 128).T)
    h["l0_bv_row"] = c(qb[2048:3072].reshape(1, 1024))
    h["l0_pw"] = c(inp["L0_attn_proj_w"])
    h["l0_pb"] = c(np.asarray(inp["L0_attn_proj_b"], f32).reshape(8, 128).T)
    h["l0_wg"] = c(inp["L0_ffn_Wg"])
    h["l0_bg16"] = c(np.tile(np.asarray(inp["L0_ffn_bg"], f32).reshape(1, 8), (1, 16)))
    h["l0_w1"] = c(inp["L0_ffn_W1"])
    h["l0_b1"] = c(np.asarray(inp["L0_ffn_b1"], f32).reshape(8, 32, 128)
                   .transpose(2, 0, 1).reshape(128, 256))
    h["l0_w2"] = c(inp["L0_ffn_W2"])
    h["l0_b2"] = c(inp["L0_ffn_b2"])
    h["l1_qw"] = c(np.asarray(inp["L1_attn_Wq"], f32).T)
    h["l1_kw"] = c(np.asarray(inp["L1_attn_Wk"], f32).T)
    h["l1_vw"] = c(np.asarray(inp["L1_attn_Wv"], f32).T)
    h["l1_pw"] = c(np.asarray(inp["L1_attn_Wo"], f32).T)
    h["l1_bq"] = c(np.asarray(inp["L1_attn_bq"], f32).reshape(8, 128).T)
    h["l1_bk"] = c(np.asarray(inp["L1_attn_bk"], f32).reshape(8, 128).T)
    h["l1_bv_row"] = c(np.asarray(inp["L1_attn_bv"], f32).reshape(1, 1024))
    h["l1_pb"] = c(np.asarray(inp["L1_attn_bo"], f32).reshape(8, 128).T)
    h["l1_wg"] = c(np.asarray(inp["L1_ffn_gate_w"], f32).T)
    h["l1_bg16"] = c(np.tile(np.asarray(inp["L1_ffn_beta"], f32).reshape(1, 8), (1, 16)))
    h["l1_w1"] = c(inp["L1_ffn_gate_proj_w"])
    h["l1_uw"] = c(inp["L1_ffn_up_proj_w"])
    h["l1_w2"] = c(inp["L1_ffn_down_proj_w"])
    h["l1_sg"] = c(np.asarray(inp["L1_ffn_shared_gate_w"], f32)[0])
    h["l1_su"] = c(np.asarray(inp["L1_ffn_shared_up_w"], f32)[0])
    h["l1_sd"] = c(np.asarray(inp["L1_ffn_shared_down_w"], f32)[0])
    h["ident"] = np.eye(128, dtype=f32)
    sel = np.zeros((8, 1024), f32)
    for e in range(8):
        sel[e, e * 128:(e + 1) * 128] = 1.0
    h["sel"] = sel
    return h


def core_thr(core):
    half = core % 2
    thr = np.zeros((TTN, NKT, 128), np.float32)
    i = np.arange(128, dtype=np.float32)
    for qt in range(TTN):
        qb = half * TOK + qt * NT
        for kt in range(NKT):
            thr[qt, kt] = i - qb + kt * 128
    return np.ascontiguousarray(thr.transpose(2, 0, 1).reshape(128, TTN * NKT))



def make_ctx(nc):
    c = Ctx()
    c.nc = nc
    a = nc.alloc_sbuf_tensor
    c.A = a("A", [128, DCH * TOK], F32)
    c.XB = a("XBuf", [128, DCH * TOK], BF16)
    c.SCR = a("SCR", [128, 16384], BF16)
    c.STG = a("STG", [128, 4096], F32)
    c.ARN = a("ARN", [128, ARNW], BF16)
    c.ONES_B = a("ONES_B", [128, 128], BF16)
    c.ONES_F = a("ONES_F", [128, 128], F32)
    c.IDENT = a("IDENT", [128, 128], F32)
    c.IDENTB = a("IDENTB", [128, 128], BF16)
    c.IOTA = a("IOTA", [128, 512], F32)
    c.IOTA2 = a("IOTA2", [128, 1024], mybir.dt.float16)
    c.THR = a("THR", [128, TTN * NKT], F32)
    c.SEL = a("SEL_sb", [8, 1024], BF16)
    c.B1 = a("B1", [128, 256], F32)
    c.B2BF = a("B2BF", [8, 1024], BF16)
    c.BQR = a("BQR", [128, 8], F32)
    c.BQ8 = a("BQ8", [128, 8], F32)
    c.BK = a("BK", [128, 8], F32)
    c.BP = a("BP", [128, 8], F32)
    c.EPS = a("EPS", [128, 1], F32)
    c.PSA = nc.alloc_psum_tensor("PSA", [128, 1024], F32)
    c.PSB = nc.alloc_psum_tensor("PSB", [128, 1024], F32)
    c.PSC = nc.alloc_psum_tensor("PSC", [128, 1024], F32)
    c.PSD = nc.alloc_psum_tensor("PSD", [128, 512], F32)
    c.PSE = nc.alloc_psum_tensor("PSE", [128, 512], F32)
    c.dram = {}
    c.input_names = set()
    return c


def din(c, name, shape, dtype=F32):
    if name not in c.dram:
        c.dram[name] = c.nc.dram_tensor(name, list(shape), dtype, kind="ExternalInput")
        c.input_names.add(name)
    return c.dram[name]


def dout(c, name, shape, dtype=F32):
    if name not in c.dram:
        c.dram[name] = c.nc.dram_tensor(name, list(shape), dtype, kind="ExternalOutput")
    return c.dram[name]


def dint(c, name, shape, dtype=BF16):
    if name not in c.dram:
        c.dram[name] = c.nc.dram_tensor(name, list(shape), dtype, kind="Internal")
    return c.dram[name]


def at(c, ch, tt):
    return c.A[:, ch * TOK + tt * NT: ch * TOK + tt * NT + NT]


def xt(c, ch, tt):
    return c.XB[:, ch * TOK + tt * NT: ch * TOK + tt * NT + NT]


def arn(c, off, w):
    return c.ARN[:, off: off + w]


def stg(c, off, w):
    return c.STG[:, off: off + w]



def ph_init(c):
    q = Sch(c)
    ih = din(c, "ident", [128, 128])
    th = din(c, "thr", [128, TTN * NKT])
    sh = din(c, "sel", [8, 1024])
    b1h = din(c, "l0_b1", [128, 256])
    b2h = din(c, "l0_b2", [8, 1024])
    mi = q.op("sync", lambda s: s.dma_start(out=c.IDENT[:], in_=ih.ap()), dma="ld0")
    q.op("sync", lambda s: s.dma_start(out=c.THR[:], in_=th.ap()), dma="ld0")
    q.op("sync", lambda s: s.dma_start(out=c.B1[:], in_=b1h.ap()), dma="ld0")
    q.op("gpsimd", lambda g: g.dma_start(out=c.SEL[:], in_=sh.ap()), dma="sw0")
    m = q.op("gpsimd", lambda g: g.dma_start(out=c.B2BF[:], in_=b2h.ap()), dma="sw0")
    q.op("gpsimd", lambda g: g.memset(c.ONES_B[:], 1.0))
    q.op("gpsimd", lambda g: g.memset(c.ONES_F[:], 1.0))
    q.op("gpsimd", lambda g: g.memset(c.EPS[:], LN_EPS))
    q.op("gpsimd", lambda g: g.iota(c.IOTA[:], [[1, 512]], channel_multiplier=0,
                                    allow_small_or_imprecise_dtypes=True))
    q.op("gpsimd", lambda g: g.iota(c.IOTA2[:, 0:512], [[1, 512]],
                                    channel_multiplier=0,
                                    allow_small_or_imprecise_dtypes=True))
    q.op("gpsimd", lambda g: g.iota(c.IOTA2[:, 512:1024], [[1, 512]],
                                    channel_multiplier=0,
                                    allow_small_or_imprecise_dtypes=True))
    q.op("gpsimd", lambda g: g.tensor_copy(out=c.IDENTB[:], in_=c.IDENT[:]),
         waits=[("sw0", m), ("ld0", c.sem["ld0"].v)])
    q.play("init")


def ph_load_x(c, name="x_fm_slice"):
    q = Sch(c)
    xh = din(c, name, [D, TOK])
    m = q.op("sync", lambda s: s.dma_start(
        out=c.A.rearrange("p (a t) -> p a t", a=DCH),
        in_=xh.rearrange("(a p) t -> p a t", p=128)), dma="ld0")
    q.op("scalar", lambda s: s.copy(out=c.XB[:], in_=c.A[:]), waits=[("ld0", m)])
    q.play("loadx")


def ph_store_stream(c, name="stream_out"):
    q = Sch(c)
    oh = dout(c, name, [D, TOK])
    m = q.op("sync", lambda s: s.dma_start(
        out=oh.rearrange("(a p) t -> p a t", p=128),
        in_=c.A.rearrange("p (a t) -> p a t", a=DCH)), dma="st0")
    q.op("sync", lambda s: s.wait_ge(c.sem["st0"].h, m))
    q.play("store_" + name)


def ph_ln(c, write_xb=True):
    q = Sch(c)
    MEANB, RINV = stg(c, 0, 512), stg(c, 512, 512)
    VAR, SD, DD = stg(c, 1024, 512), stg(c, 1536, 512), stg(c, 2048, 512)
    castm, statm, consm = {}, {}, {}
    for tt in range(TTN):
        for ch in range(DCH):
            k = tt * DCH + ch
            sl = (k % 2) * 512
            w = [("mm", statm[k - 2])] if k >= 2 else []
            q.op("scalar", lambda s, ch=ch, tt=tt, sl=sl:
                 s.copy(out=arn(c, sl, 512), in_=at(c, ch, tt)), waits=w)
            castm[k] = q.op("scalar", lambda s, ch=ch, tt=tt, sl=sl:
                            s.activation(out=arn(c, 1024 + sl, 512),
                                         in_=at(c, ch, tt), func=AF.Square))
            w2 = [("sc", castm[k])]
            if ch == 0 and tt >= 1:
                w2.append(("ev", consm[tt - 1]))
            q.op("tensor", lambda t, sl=sl, ch=ch:
                 t.matmul(c.PSA[:, 0:512], c.ONES_B[:], arn(c, sl, 512),
                          start=(ch == 0), stop=(ch == DCH - 1)), waits=w2)
            statm[k] = q.op("tensor", lambda t, sl=sl, ch=ch:
                            t.matmul(c.PSA[:, 512:1024], c.ONES_B[:],
                                     arn(c, 1024 + sl, 512),
                                     start=(ch == 0), stop=(ch == DCH - 1)),
                            inc="mm")
        q.op("vector", lambda v: v.tensor_scalar(
            out=MEANB, in0=c.PSA[:, 0:512], scalar1=1.0 / D, scalar2=None,
            op0=ALU.mult), waits=[("mm", statm[tt * DCH + DCH - 1])])
        q.op("vector", lambda v: v.tensor_scalar(
            out=VAR, in0=c.PSA[:, 512:1024], scalar1=1.0 / D, scalar2=None,
            op0=ALU.mult))
        q.op("vector", lambda v: v.tensor_tensor(out=DD, in0=MEANB, in1=MEANB,
                                                 op=ALU.mult))
        consm[tt] = q.op("vector", lambda v: v.tensor_tensor(
            out=VAR, in0=VAR, in1=DD, op=ALU.subtract))
        sq = q.op("scalar", lambda s: s.activation(out=SD, in_=VAR, func=AF.Sqrt,
                                                   bias=c.EPS[:]),
                  waits=[("ev", consm[tt])])
        q.op("vector", lambda v: v.reciprocal(out=RINV, in_=SD), waits=[("sc", sq)])
        for ch in range(DCH):
            q.op("vector", lambda v, ch=ch, tt=tt: v.tensor_tensor(
                out=DD, in0=at(c, ch, tt), in1=MEANB, op=ALU.subtract))
            q.op("vector", lambda v, ch=ch, tt=tt: v.tensor_tensor(
                out=at(c, ch, tt), in0=DD, in1=RINV, op=ALU.mult))
            if write_xb:
                q.op("vector", lambda v, ch=ch, tt=tt: v.tensor_tensor(
                    out=xt(c, ch, tt), in0=DD, in1=RINV, op=ALU.mult))
    q.play("ln")



def ph_attn_pre(c, L, q=None):
    standalone = q is None
    if standalone:
        q = Sch(c)
    kwh = din(c, f"l{L}_kw", [D, D])
    vwh = din(c, f"l{L}_vw", [D, D])
    bkh = din(c, f"l{L}_bk", [128, 8])
    bqh = din(c, f"l{L}_bq", [128, 8])
    bvh = din(c, f"l{L}_bv_row", [1, 1024])
    kin = dint(c, "kin", [D, TOK])
    vin = dint(c, "vin", [TOK, D])
    kout = dint(c, "kout", [2 * D, TOK])
    vout = dint(c, "vout", [SEQ, D])
    BVB = stg(c, 2048, 1024)
    ROWS = c.STG[0:1, 3072:4096]

    def vslot(tc):
        return c.SCR[:, 4096 + (tc % 2) * 1024: 4096 + (tc % 2) * 1024 + 1024]

    def kslot(oc):
        return c.SCR[:, (oc % 2) * TOK: (oc % 2) * TOK + TOK]

    q.op("sync", lambda s: s.dma_start(out=c.BK[:], in_=bkh.ap()), dma="ld0")
    q.op("sync", lambda s: s.dma_start(out=c.BQR[:], in_=bqh.ap()), dma="ld0")
    mrow = q.op("sync", lambda s: s.dma_start(out=ROWS[0:1, 0:1024], in_=bvh.ap()),
                dma="ld0")
    c.bias_mark = mrow
    kwm = q.op("gpsimd", lambda g: g.dma_start(
        out=c.ARN[:, 0:8192].rearrange("p (a t) -> p a t", a=DCH),
        in_=kwh.rearrange("(a p) t -> p a t", p=128)), dma="sw1")
    vwm = q.op("gpsimd", lambda g: g.dma_start(
        out=c.ARN[:, 8192:16384].rearrange("p (a t) -> p a t", a=DCH),
        in_=vwh.rearrange("(a p) t -> p a t", p=128)), dma="sw2")
    q.op("tensor", lambda t: t.matmul(c.PSC[:, 0:512], c.ONES_F[0:1, :],
                                      ROWS[0:1, 0:512], start=True, stop=True),
         waits=[("ld0", mrow)])
    mb = q.op("tensor", lambda t: t.matmul(c.PSC[:, 512:1024], c.ONES_F[0:1, :],
                                           ROWS[0:1, 512:1024], start=True,
                                           stop=True), inc="mm")
    bvbm = q.op("scalar", lambda s: s.copy(out=BVB, in_=c.PSC[:, 0:1024]),
                waits=[("mm", mb)])
    q.op("scalar", lambda s: s.mul(out=c.BQ8[:], in_=c.BQR[:], mul=0.125))
    vevm, vstm = {}, {}
    for tc in range(16):
        pl = c.PSD[:] if tc % 2 == 0 else c.PSC[:, 0:512]
        pr = c.PSE[:] if tc % 2 == 0 else c.PSC[:, 512:1024]
        w = [("sw2", vwm)]
        if tc == 1:
            w.append(("sc", bvbm))
        if tc >= 2:
            w.append(("ev", vevm[tc - 2]))
        mlast = None
        for k in range(DCH):
            xs = c.XB[:, k * TOK + tc * 128: k * TOK + tc * 128 + 128]
            q.op("tensor", lambda t, xs=xs, k=k, pl=pl: t.matmul(
                pl, xs, arn(c, 8192 + k * 1024, 512),
                start=(k == 0), stop=(k == DCH - 1)), waits=(w if k == 0 else ()))
            mlast = q.op("tensor", lambda t, xs=xs, k=k, pr=pr: t.matmul(
                pr, xs, arn(c, 8192 + k * 1024 + 512, 512),
                start=(k == 0), stop=(k == DCH - 1)),
                inc=("mm" if k == DCH - 1 else None))
        w2 = [("mm", mlast), ("sc", bvbm)]
        if tc >= 2:
            w2.append((f"st{tc % 2}", vstm[tc - 2]))
        q.op("vector", lambda v, tc=tc, pl=pl: v.tensor_tensor(
            out=vslot(tc)[:, 0:512], in0=pl, in1=BVB[:, 0:512], op=ALU.add),
            waits=w2)
        vevm[tc] = q.op("vector", lambda v, tc=tc, pr=pr: v.tensor_tensor(
            out=vslot(tc)[:, 512:1024], in0=pr, in1=BVB[:, 512:1024], op=ALU.add))
        vstm[tc] = q.op("sync", lambda s, tc=tc: s.dma_start(
            out=vin.ap()[tc * 128:(tc + 1) * 128, :], in_=vslot(tc)),
            waits=[("ev", vevm[tc])], dma=f"st{tc % 2}")
    kevm, kstm = {}, {}
    for oc in range(DCH):
        for tt in range(TTN):
            k = oc * TTN + tt
            ps = c.PSA[:, 0:512] if k % 2 == 0 else c.PSB[:, 0:512]
            w = [("sw1", kwm)]
            if k >= 2:
                w.append(("ev", kevm[k - 2]))
            mlast = None
            for kk in range(DCH):
                mlast = q.op("tensor", lambda t, kk=kk, oc=oc, tt=tt, ps=ps:
                             t.matmul(ps, arn(c, kk * 1024 + oc * 128, 128),
                                      xt(c, kk, tt),
                                      start=(kk == 0), stop=(kk == DCH - 1)),
                             waits=(w if kk == 0 else ()),
                             inc=("mm" if kk == DCH - 1 else None))
            w2 = [("mm", mlast), ("ld0", mrow)]
            if oc >= 2 and tt == 0:
                w2.append((f"st{oc % 2}", kstm[oc - 2]))
            kevm[k] = q.op("vector", lambda v, oc=oc, tt=tt, ps=ps:
                           v.tensor_scalar(out=kslot(oc)[:, tt * NT: tt * NT + NT],
                                           in0=ps, scalar1=c.BK[:, oc:oc + 1],
                                           scalar2=None, op0=ALU.add), waits=w2)
        kstm[oc] = q.op("sync", lambda s, oc=oc: s.dma_start(
            out=kin.ap()[oc * 128:(oc + 1) * 128, :], in_=kslot(oc)),
            waits=[("ev", kevm[oc * TTN + TTN - 1])], dma=f"st{oc % 2}")
    ccm = q.op("gpsimd", lambda g: g.collective_compute(
        "AllGather", ALU.bypass, replica_groups=GROUPS,
        ins=[kin.ap().opt()], outs=[kout.ap().opt()]),
        waits=[("st0", kstm[DCH - 2]), ("st1", kstm[DCH - 1]),
               ("st0", vstm[14]), ("st1", vstm[15])], inc="cc")
    q.op("gpsimd", lambda g: g.collective_compute(
        "AllGather", ALU.bypass, replica_groups=GROUPS,
        ins=[vin.ap().opt()], outs=[vout.ap().opt()]),
        waits=[("cc", ccm)], inc="cc")
    c.pre_kev = kevm[DCH * TTN - 1]
    c.pre_vev = vevm[15]
    if standalone:
        q.play(f"attnpre{L}")


def ph_attn_main(c, L):
    q = Sch(c)
    ph_attn_pre(c, L, q=q)
    causal = (L == 1)
    qwh = din(c, f"l{L}_qw", [D, D])
    kout = dint(c, "kout", [2 * D, TOK])
    vout = dint(c, "vout", [SEQ, D])
    vre = vout.rearrange("(a p) d -> p a d", p=128)
    LT = arn(c, 22144, 1024)
    LBC = stg(c, 2048, 1024)
    QW = 20608
    TMPB = arn(c, 21632, 512)
    ccv = c.sem["cc"].v

    def vb(buf):
        return c.ARN[:, 4096 + buf * 4160: 4096 + (buf + 1) * 4160].rearrange(
            "p (a g) -> p a g", g=130)

    def khp(buf, kt):
        return c.ARN[:, kt * 128: kt * 128 + 128]

    def qhp(buf, qt):
        return c.ARN[:, 12416 + buf * TOK + qt * NT: 12416 + buf * TOK + qt * NT + NT]

    def ptbase(kt):
        sl = kt % 5
        return 16512 + sl * 1024 if sl < 4 else 23168

    def pt(kt, head):
        o = ptbase(kt) + head * 512
        return c.ARN[:, o: o + 512]

    q.op("gpsimd", lambda g: g.memset(
        c.ARN[:, 4096:12416].rearrange("p (a g) -> p a g", g=130)[:, :, 64:65], 1.0))
    vones = q.op("gpsimd", lambda g: g.memset(
        c.ARN[:, 4096:12416].rearrange("p (a g) -> p a g", g=130)[:, :, 129:130], 1.0))

    qcast, kvld = {}, {}
    qpm, qem, sm, xm, mkm, avm, rm, bcm, lbcm, oam, shm, obm = ({} for _ in range(12))
    for hp in range(NHP):
        buf = hp % 2
        w = [] if hp == 0 else [("mm", qpm[(hp - 1, TTN - 1)])]
        qcast[hp] = q.op("gpsimd", lambda g, hp=hp: g.dma_start(
            out=arn(c, QW, 1024).rearrange("p (a t) -> p a t", a=DCH),
            in_=qwh.rearrange("(a p) t -> p a t", p=128)
            [:, :, hp * 128:(hp + 1) * 128]), waits=w, dma="sw3")
        w = [("mm", shm[(hp - 2, TTN - 1)])] if hp >= 2 else []
        if hp == 0:
            w.append(("cc", ccv))
        if hp >= 1:
            w.append(("mm", sm[(hp - 1, TTN - 1, NKT - 1)]))
        kv = f"ld{1 + hp % 2}"
        q.op("sync", lambda s, hp=hp, buf=buf: s.dma_start(
            out=c.ARN[:, 0:TOK],
            in_=kout.ap()[hp * 128:(hp + 1) * 128, :]), waits=w, dma=kv)
        q.op("sync", lambda s, hp=hp, buf=buf: s.dma_start(
            out=c.ARN[:, TOK:SEQ],
            in_=kout.ap()[D + hp * 128: D + hp * 128 + 128, :]), dma=kv)
        q.op("sync", lambda s, hp=hp, buf=buf: s.dma_start(
            out=vb(buf)[:, :, 0:64], in_=vre[:, :, hp * 128: hp * 128 + 64]),
            dma=kv)
        kvld[hp] = q.op("sync", lambda s, hp=hp, buf=buf: s.dma_start(
            out=vb(buf)[:, :, 65:129],
            in_=vre[:, :, hp * 128 + 64: hp * 128 + 128]), dma=kv)
        for tt in range(TTN):
            ps = c.PSA[:, 0:512] if tt % 2 == 0 else c.PSB[:, 0:512]
            w = [("sw3", qcast[hp])]
            if hp == 0 and tt == 0:
                w.append(("ld0", c.bias_mark))
                w.append(("ev", c.pre_kev))
            if hp >= 1 and tt == 0:
                w.append(("sc", xm[(hp - 1, TTN - 1, NKT - 1)]))
            if tt >= 2:
                w.append(("ev", qem[(hp, tt - 2)]))
            for kk in range(DCH):
                qpm[(hp, tt)] = q.op("tensor", lambda t, kk=kk, tt=tt, ps=ps:
                                     t.matmul(ps, arn(c, QW + kk * 128, 128),
                                              xt(c, kk, tt), start=(kk == 0),
                                              stop=(kk == DCH - 1)),
                                     waits=(w if kk == 0 else ()),
                                     inc=("mm" if kk == DCH - 1 else None))
            qem[(hp, tt)] = q.op("vector", lambda v, tt=tt, ps=ps, buf=buf, hp=hp:
                                 v.tensor_scalar(out=qhp(buf, tt), in0=ps,
                                                 scalar1=c.BQR[:, hp:hp + 1],
                                                 scalar2=0.125, op0=ALU.add,
                                                 op1=ALU.mult),
                                 waits=[("mm", qpm[(hp, tt)])])
        for qt in range(TTN):
            LAG = 4
            for step in range(NKT + LAG):
                if step < NKT:
                    kt = step
                    sps = (c.PSA, c.PSB, c.PSC)[kt % 3]
                    w = [(f"ld{1 + hp % 2}", kvld[hp])]
                    if (qt, kt) == (0, 0):
                        w.append(("ev", qem[(hp, TTN - 1)]))
                        w.append(("gp", vones))
                    if kt >= 3:
                        w.append(("sc", xm[(hp, qt, kt - 3)]))
                    elif qt >= 1:
                        w.append(("sc", xm[(hp, qt - 1, NKT - 3 + kt)]))
                    if kt % 3 == 2 and (kt == 2 or (qt, kt) == (0, 2)) and \
                            lbcm.get((hp, qt - 1) if qt >= 1 else
                                     (hp - 1, TTN - 1)) is not None:
                        pq = (hp, qt - 1) if qt >= 1 else (hp - 1, TTN - 1)
                        w.append(("sc", lbcm[pq]))
                    q.op("tensor", lambda t, kt=kt, qt=qt, sps=sps, buf=buf:
                         t.matmul(sps[:, 0:512], khp(buf, kt)[0:64, :],
                                  qhp(buf, qt)[0:64, :],
                                  start=True, stop=True, tile_position=(0, 0)),
                         waits=w)
                    sm[(hp, qt, kt)] = q.op(
                        "tensor", lambda t, kt=kt, qt=qt, sps=sps, buf=buf:
                        t.matmul(sps[:, 512:1024], khp(buf, kt)[64:128, :],
                                 qhp(buf, qt)[64:128, :], start=True, stop=True,
                                 tile_position=(64, 0)), inc="mm")
                    w = [("mm", sm[(hp, qt, kt)])]
                    if kt >= 5:
                        w.append(("mm", avm[(hp, qt, kt - 5)]))
                    elif qt >= 1:
                        w.append(("mm", avm[(hp, qt - 1, NKT - 5 + kt)]))
                    elif hp >= 1:
                        w.append(("mm", avm[(hp - 1, TTN - 1, NKT - 5 + kt)]))
                    xm[(hp, qt, kt)] = q.op("scalar", lambda s, kt=kt, sps=sps:
                                            s.activation(
                        out=c.ARN[:, ptbase(kt): ptbase(kt) + 1024],
                        in_=sps[:, 0:1024], func=AF.Exp), waits=w, chain=False)
                    if causal:
                        tc_ = c.THR[:, qt * NKT + kt: qt * NKT + kt + 1]
                        pb_ = c.ARN[:, ptbase(kt): ptbase(kt) + 1024]
                        mkm[(hp, qt, kt)] = q.op(
                            "vector", lambda v, pb_=pb_, tc_=tc_:
                            v.scalar_tensor_tensor(
                                out=pb_, in0=c.IOTA2[:], scalar=tc_,
                                in1=pb_, op0=ALU.is_ge, op1=ALU.mult),
                            waits=[("sc", xm[(hp, qt, kt)])], chain=False)
                if step >= LAG:
                    kt = step - LAG
                    if causal:
                        wav = [("ev", mkm[(hp, qt, kt)])]
                    else:
                        wav = [("sc", xm[(hp, qt, kt)])]
                    if qt >= 1 and kt == 0:
                        wav.append(("sc", obm[(hp, qt - 1)]))
                    elif hp >= 1 and qt == 0 and kt == 0:
                        wav.append(("sc", obm[(hp - 1, TTN - 1)]))
                    elif (hp, qt, kt) == (0, 0, 0):
                        wav.append(("ev", c.pre_vev))
                    q.op("tensor", lambda t, kt=kt, buf=buf: t.matmul(
                        c.PSD[0:65, :], vb(buf)[:, kt, 0:65], pt(kt, 0),
                        start=(kt == 0), stop=(kt == NKT - 1)), waits=wav)
                    avm[(hp, qt, kt)] = q.op("tensor", lambda t, kt=kt, buf=buf:
                                             t.matmul(
                        c.PSE[0:65, :], vb(buf)[:, kt, 65:130], pt(kt, 1),
                        start=(kt == 0), stop=(kt == NKT - 1)), inc="mm")
            def _recA(v):
                with c.nc.allow_low_precision(
                        reason="1/l broadcast fed to a bf16 matmul operand"):
                    return v.reciprocal(out=LT[64:65, 0:512], in_=c.PSD[64:65, :])

            def _recB(v):
                with c.nc.allow_low_precision(
                        reason="1/l broadcast fed to a bf16 matmul operand"):
                    return v.reciprocal(out=LT[64:65, 512:1024], in_=c.PSE[64:65, :])

            q.op("vector", _recA, waits=[("mm", avm[(hp, qt, NKT - 1)])])
            rm[(hp, qt)] = q.op("vector", _recB)
            w = [("ev", rm[(hp, qt)])]
            if (hp, qt) != (0, 0):
                pq = (hp, qt - 1) if qt >= 1 else (hp - 1, TTN - 1)
                w.append(("sc", lbcm[pq]))
            q.op("tensor", lambda t: t.matmul(
                c.PSC[:, 0:512], c.ONES_B[64:65, :], LT[64:65, 0:512],
                start=True, stop=True), waits=w)
            bcm[(hp, qt)] = q.op("tensor", lambda t: t.matmul(
                c.PSC[:, 512:1024], c.ONES_B[64:65, :], LT[64:65, 512:1024],
                start=True, stop=True), inc="mm")
            w = [("mm", bcm[(hp, qt)])]
            if (hp, qt) != (0, 0):
                pq = (hp, qt - 1) if qt >= 1 else (hp - 1, TTN - 1)
                w.append(("ev", oam[pq]))
            lbcm[(hp, qt)] = q.op("scalar", lambda s: s.copy(
                out=LBC, in_=c.PSC[:, 0:1024]), waits=w)
            och = c.SCR[:, hp * TOK + qt * NT: hp * TOK + qt * NT + NT]
            q.op("vector", lambda v, och=och: v.tensor_tensor(
                out=och[0:64, :], in0=c.PSD[0:64, :], in1=LBC[0:64, 0:512],
                op=ALU.mult), waits=[("sc", lbcm[(hp, qt)])])
            oam[(hp, qt)] = q.op("vector", lambda v: v.tensor_tensor(
                out=TMPB[0:64, :], in0=c.PSE[0:64, :], in1=LBC[0:64, 512:1024],
                op=ALU.mult))
            shm[(hp, qt)] = q.op("tensor", lambda t: t.matmul(
                c.PSD[64:128, :], c.IDENTB[0:64, 0:64], TMPB[0:64, :],
                start=True, stop=True, tile_position=(0, 64)),
                waits=[("ev", oam[(hp, qt)])], inc="mm")
            obm[(hp, qt)] = q.op("scalar", lambda s, och=och: s.copy(
                out=och[64:128, :], in_=c.PSD[64:128, :]),
                waits=[("mm", shm[(hp, qt)])])
    q.play(f"attnmain{L}")


def ph_attn_proj(c, L):
    q = Sch(c)
    pwh = din(c, f"l{L}_pw", [D, D])
    pbh = din(c, f"l{L}_pb", [128, 8])
    mpb = q.op("sync", lambda s: s.dma_start(out=c.BP[:], in_=pbh.ap()), dma="ld0")
    pwm = q.op("gpsimd", lambda g: g.dma_start(
        out=c.ARN[:, 0:8192].rearrange("p (a t) -> p a t", a=DCH),
        in_=pwh.rearrange("(a p) t -> p a t", p=128)), dma="sw1")
    ym, evm = {}, {}
    for do in range(DCH):
        for tt in range(TTN):
            k = do * TTN + tt
            ps = c.PSD[:] if k % 2 == 0 else c.PSE[:]
            w = [("sw1", pwm), ("ld0", mpb)]
            if k >= 2:
                w.append(("ev", evm[k - 2]))
            for hp in range(NHP):
                ym[k] = q.op("tensor", lambda t, hp=hp, do=do, tt=tt, ps=ps:
                             t.matmul(ps, arn(c, hp * 1024 + do * 128, 128),
                                      c.SCR[:, hp * TOK + tt * NT:
                                            hp * TOK + tt * NT + NT],
                                      start=(hp == 0), stop=(hp == NHP - 1)),
                             waits=(w if hp == 0 else ()),
                             inc=("mm" if hp == NHP - 1 else None))
            evm[k] = q.op("vector", lambda v, do=do, tt=tt, ps=ps:
                          v.scalar_tensor_tensor(
                out=at(c, do, tt), in0=ps, scalar=c.BP[:, do:do + 1],
                in1=at(c, do, tt), op0=ALU.add, op1=ALU.add),
                waits=[("mm", ym[k])])
    q.play(f"attnproj{L}")



def ph_gate(c, L):
    q = Sch(c)
    wgh = din(c, f"l{L}_wg", [D, 8])
    rowh = din(c, f"l{L}_bg16", [1, 128])
    ZT, WT = stg(c, 2048, 128), stg(c, 2176, 128)
    M8, ZAP = stg(c, 2304, 128), stg(c, 2432, 128)
    MSK, QG, CT = stg(c, 2560, 128), stg(c, 2688, 128), stg(c, 2816, 128)
    Q16, D16, R16, T16 = (stg(c, 2944, 16), stg(c, 2960, 16),
                          stg(c, 2976, 16), stg(c, 2992, 16))
    ROWS = c.STG[0:1, 3072:3200]
    CFM = c.ARN[0:8, 18432:20480]

    ldw = q.op("sync", lambda s: s.dma_start(
        out=c.STG[:, 0:64].rearrange("p (a t) -> p a t", a=DCH, t=8),
        in_=wgh.rearrange("(a p) t -> p a t", p=128)), dma="ld0")
    ldr = q.op("sync", lambda s: s.dma_start(out=ROWS[0:1, :], in_=rowh.ap()),
               dma="ld0")
    gw = q.op("gpsimd", lambda g: g.tensor_copy(out=c.ARN[:, 0:64],
                                                in_=c.STG[:, 0:64]),
              waits=[("ld0", ldr)])
    bb = q.op("tensor", lambda t: t.matmul(c.PSB[:, 0:128], c.ONES_F[0:1, :],
                                           ROWS[0:1, :], start=True, stop=True),
              waits=[("ld0", ldr)], inc="mm")
    zm = None
    for tc in range(16):
        for kk in range(DCH):
            zm = q.op("tensor", lambda t, tc=tc, kk=kk: t.matmul(
                c.PSA[:, tc * 8: tc * 8 + 8],
                c.XB[:, kk * TOK + tc * 128: kk * TOK + tc * 128 + 128],
                c.ARN[:, kk * 8: kk * 8 + 8],
                start=(kk == 0), stop=(kk == DCH - 1)),
                waits=([("gp", gw)] if (tc, kk) == (0, 0) else ()),
                inc=("mm" if (tc, kk) == (15, DCH - 1) else None))
    q.op("vector", lambda v: v.tensor_copy(out=ZT, in_=c.PSA[:, 0:128]),
         waits=[("mm", zm), ("mm", bb)])
    q.op("vector", lambda v: v.tensor_tensor(out=WT, in0=ZT,
                                             in1=c.PSB[:, 0:128], op=ALU.add))
    for tc in range(16):
        q.op("vector", lambda v, tc=tc: v.max(out=M8[:, tc * 8: tc * 8 + 8],
                                              in_=WT[:, tc * 8: tc * 8 + 8]))
    if L == 0:
        q.op("scalar", lambda s: s.activation(
            out=D16.rearrange("p (a o) -> p a o", o=1),
            in_=M8.rearrange("p (a g) -> p a g", g=8)[:, :, 0:1], func=AF.Exp),
            waits=[("ev", c.sem["ev"].v)])
        q.op("scalar", lambda s: s.activation(
            out=R16.rearrange("p (a o) -> p a o", o=1),
            in_=M8.rearrange("p (a g) -> p a g", g=8)[:, :, 1:2], func=AF.Exp))
        sgq = q.op("scalar", lambda s: s.activation(out=QG, in_=WT, func=AF.Exp))
    else:
        sgq = q.op("scalar", lambda s: s.activation(out=QG, in_=ZT,
                                                    func=AF.Sigmoid),
                   waits=[("ev", c.sem["ev"].v)])
    q.op("vector", lambda v: v.memset(
        M8.rearrange("p (a g) -> p a g", g=8)[:, :, 2:8], -2.0e30),
        waits=[("sc", sgq)])
    for tc in range(16):
        q.op("vector", lambda v, tc=tc: v.match_replace(
            out=ZAP[:, tc * 8: tc * 8 + 8], in_to_replace=M8[:, tc * 8: tc * 8 + 8],
            in_values=WT[:, tc * 8: tc * 8 + 8], imm_value=-2.0e30))
    q.op("vector", lambda v: v.tensor_tensor(out=MSK, in0=WT, in1=ZAP,
                                             op=ALU.not_equal))
    ctm = q.op("vector", lambda v: v.tensor_tensor(out=CT, in0=QG, in1=MSK,
                                                   op=ALU.mult))
    if L == 0:
        q.op("vector", lambda v: v.reduce_sum(
            Q16.rearrange("p (a o) -> p a o", o=1),
            QG.rearrange("p (a g) -> p a g", g=8), axis=AX.X))
        q.op("vector", lambda v: v.tensor_tensor(out=D16, in0=D16, in1=R16,
                                                 op=ALU.add))
        q.op("vector", lambda v: v.tensor_scalar(out=T16, in0=Q16, scalar1=1e-8,
                                                 scalar2=None, op0=ALU.mult))
        q.op("vector", lambda v: v.tensor_tensor(out=D16, in0=D16, in1=T16,
                                                 op=ALU.add))
        q.op("vector", lambda v: v.reciprocal(out=R16, in_=D16))
        for tc in range(16):
            ctm = q.op("vector", lambda v, tc=tc: v.tensor_scalar(
                out=CT[:, tc * 8:tc * 8 + 8], in0=CT[:, tc * 8:tc * 8 + 8],
                scalar1=R16[:, tc:tc + 1], scalar2=None, op0=ALU.mult))
    trm = {}
    for tc in range(16):
        ps = c.PSA if tc < 8 else c.PSB
        trm[tc] = q.op("tensor", lambda t, tc=tc, ps=ps: t.transpose(
            ps[0:8, (tc % 8) * 128: (tc % 8) * 128 + 128],
            CT[:, tc * 8: tc * 8 + 8], c.IDENT[:]),
            waits=([("ev", ctm)] if tc == 0 else ()),
            inc=("mm" if tc in (7, 15) else None))
    q.op("scalar", lambda s: s.copy(out=CFM[0:8, 0:1024], in_=c.PSA[0:8, :]),
         waits=[("mm", trm[7])])
    q.op("scalar", lambda s: s.copy(out=CFM[0:8, 1024:2048], in_=c.PSB[0:8, :]),
         waits=[("mm", trm[15])])
    q.play(f"gate{L}")


def ph_b2c(c):
    q = Sch(c)
    CFM = c.ARN[0:8, 18432:20480]
    evm = {}
    for do in range(DCH):
        for tt in range(TTN):
            k = do * TTN + tt
            ps = c.PSD[:] if k % 2 == 0 else c.PSE[:]
            w = [("ev", evm[k - 2])] if k >= 2 else []
            m = q.op("tensor", lambda t, do=do, tt=tt, ps=ps: t.matmul(
                ps, c.B2BF[0:8, do * 128:(do + 1) * 128],
                CFM[0:8, tt * NT: tt * NT + NT], start=True, stop=True),
                waits=w, inc="mm")
            evm[k] = q.op("vector", lambda v, do=do, tt=tt, ps=ps:
                          v.tensor_tensor(out=at(c, do, tt), in0=at(c, do, tt),
                                          in1=ps, op=ALU.add),
                          waits=[("mm", m)])
    q.play("b2c")



def ph_experts(c, L):
    q = Sch(c)
    CFM = c.ARN[0:8, 18432:20480]
    CBC = arn(c, 16384, 2048)

    if L == 0:
        NE, NFS, FCS, FSW = 8, 4, 8, 1024
        w1h = din(c, "l0_w1", [8, D, 4096])
        w2h = din(c, "l0_w2", [8, 4096, D])
        uwh = sgh = suh = sdh = None
    else:
        NE, NFS, FCS, FSW = 9, 4, 4, 512
        w1h = din(c, "l1_w1", [8, D, 2048])
        uwh = din(c, "l1_uw", [8, D, 2048])
        w2h = din(c, "l1_w2", [8, 2048, D])
        sgh = din(c, "l1_sg", [D, 2048])
        suh = din(c, "l1_su", [D, 2048])
        sdh = din(c, "l1_sd", [2048, D])

    def w1slice(e, fs):
        src = w1h.ap()[e] if (L == 0 or e < 8) else sgh.ap()
        return src.rearrange("(a p) f -> p a f", p=128)[:, :, fs * FSW:(fs + 1) * FSW]

    def wuslice(e, fs):
        src = uwh.ap()[e] if e < 8 else suh.ap()
        return src.rearrange("(a p) f -> p a f", p=128)[:, :, fs * FSW:(fs + 1) * FSW]

    def w2slice(e, fs):
        src = w2h.ap()[e] if (L == 0 or e < 8) else sdh.ap()
        return src[fs * FSW:(fs + 1) * FSW, :].rearrange("(a p) f -> p a f", p=128)

    def hsc(fc, tt):
        return c.SCR[:, fc * 2048 + tt * 512: fc * 2048 + tt * 512 + 512]

    p1last = {}
    p2last = {}
    castm = {}
    cbcm = {}
    hsm_all = {}
    hvm_all = {}
    yem_all = {}

    LSEM = {"w1": "sw1", "wu": "sw3", "w2": "sw2"}

    def emit_loads(sl, e, fs):
        if L == 0:
            srcs = [("w1", w1slice(e, fs), 0), ("w2", w2slice(e, fs), 8192)]
        else:
            srcs = [("w1", w1slice(e, fs), 0), ("wu", wuslice(e, fs), 4096),
                    ("w2", w2slice(e, fs), 8192)]
        for kind, src, dsto in srcs:
            ncnk, width = src.shape[1], src.shape[2]
            w = []
            if sl >= 1:
                if kind in ("w1", "wu"):
                    w.append(("mm", p1last[sl - 1]))
                else:
                    w.append(("mm", p2last[sl - 1]))
            m = q.op("gpsimd", lambda g, src=src, dsto=dsto, ncnk=ncnk,
                     width=width: g.dma_start(
                out=arn(c, dsto, ncnk * width).rearrange(
                    "p (a t) -> p a t", a=ncnk), in_=src),
                waits=w, dma=LSEM[kind])
            castm[(sl, kind)] = (LSEM[kind], m)

    for e in range(NE):
        if e < 8:
            sl0 = e * NFS
            for pair in range(2):
                ms = None
                for tt in range(2 * pair, 2 * pair + 2):
                    w = []
                    if L == 1 and sl0 >= 1 and tt == 2 * pair:
                        w.append(("ev", hvm_all[(sl0 - 1, FCS * TTN - 1)]))
                    if pair == 1 and tt == 2:
                        w.append(("sc", cbcm[e]))
                    ms = q.op("tensor", lambda t, tt=tt, e=e: t.matmul(
                        c.PSC[:, (tt % 2) * 512: (tt % 2) * 512 + 512],
                        c.SEL[0:8, e * 128:(e + 1) * 128],
                        CFM[0:8, tt * NT: tt * NT + NT], start=True, stop=True),
                        waits=w, inc=("mm" if tt % 2 == 1 else None))
                w = [("mm", ms)]
                if e >= 1 and pair == 0:
                    w.append(yem_all[(sl0 - 1, DCH * TTN - 1)])
                q.op("scalar", lambda s, pair=pair: s.copy(
                    out=CBC[:, pair * 1024: pair * 1024 + 512],
                    in_=c.PSC[:, 0:512]), waits=w)
                cbcm[e] = q.op("scalar", lambda s, pair=pair: s.copy(
                    out=CBC[:, pair * 1024 + 512: pair * 1024 + 1024],
                    in_=c.PSC[:, 512:1024]))

        for fs in range(NFS):
            sl = e * NFS + fs
            emit_loads(sl, e, fs)
            hm, hsm, hvm, ym, yem = {}, {}, {}, {}, {}
            for fc in range(FCS):
                for tt in range(TTN):
                    k = fc * TTN + tt
                    pg = c.PSD[:] if k % 2 == 0 else c.PSE[:]
                    pu = c.PSC[:, 0:512] if k % 2 == 0 else c.PSC[:, 512:1024]
                    w = []
                    if k == 0:
                        w.append(castm[(sl, "w1")])
                        if L == 1:
                            w.append(castm[(sl, "wu")])
                            if e < 8:
                                w.append(("sc", cbcm[e]))
                    if k >= 2:
                        w.append((("sc", hsm[k - 2]) if L == 0
                                  else ("ev", hvm[k - 2])))
                    elif sl >= 1:
                        kk2 = k - 2 + FCS * TTN
                        w.append((("sc", hsm_all[(sl - 1, kk2)]) if L == 0
                                  else ("ev", hvm_all[(sl - 1, kk2)])))
                        w.append(yem_all[(sl - 1, 28 + k)])
                    for kk in range(DCH):
                        hm[k] = q.op("tensor", lambda t, kk=kk, fc=fc, tt=tt,
                                     pg=pg: t.matmul(
                            pg, arn(c, kk * FSW + fc * 128, 128), xt(c, kk, tt),
                            start=(kk == 0), stop=(kk == DCH - 1)),
                            waits=(w if kk == 0 else ()),
                            inc=("mm" if (L == 0 and kk == DCH - 1) else None))
                        if L == 1:
                            hm[k] = q.op("tensor", lambda t, kk=kk, fc=fc, tt=tt,
                                         pu=pu: t.matmul(
                                pu, arn(c, 4096 + kk * FSW + fc * 128, 128),
                                xt(c, kk, tt),
                                start=(kk == 0), stop=(kk == DCH - 1)),
                                inc=("mm" if kk == DCH - 1 else None))
                    if L == 0:
                        bias = c.B1[:, e * 32 + fs * 8 + fc:
                                    e * 32 + fs * 8 + fc + 1]
                        hsm[k] = q.op("scalar", lambda s, fc=fc, tt=tt, pg=pg,
                                      bias=bias: s.activation(
                            out=hsc(fc, tt), in_=pg, func=AF.Gelu_apprx_tanh,
                            bias=bias), waits=[("mm", hm[k])])
                    else:
                        w2_ = [("mm", hm[k])]
                        if k >= 2:
                            w2_.append(("ev", hvm[k - 2]))
                        hsm[k] = q.op("scalar", lambda s, k=k, pg=pg:
                                      s.activation(
                            out=stg(c, 2560 + (k % 2) * 512, 512), in_=pg,
                            func=AF.Silu), waits=w2_)
                        hvm[k] = q.op("vector", lambda v, fc=fc, tt=tt, k=k,
                                      pu=pu: v.tensor_tensor(
                            out=hsc(fc, tt), in0=stg(c, 2560 + (k % 2) * 512, 512),
                            in1=pu, op=ALU.mult), waits=[("sc", hsm[k])])
                    hsm_all[(sl, k)] = hsm[k]
                    if L == 1:
                        hvm_all[(sl, k)] = hvm[k]
            YB = [c.PSA[:, 0:512], c.PSA[:, 512:1024], c.PSB[:, 0:512],
                  c.PSB[:, 512:1024], c.PSD[:], c.PSE[:]]
            PREVK = [30, 31, 26, 27, 28, 29]
            for do in range(DCH):
                for tt in range(TTN):
                    k = do * TTN + tt
                    ps = YB[k % 6]
                    w = []
                    if k == 0:
                        w.append(castm[(sl, "w2")])
                    if k >= 6:
                        w.append(yem[k - 6])
                    elif sl >= 1:
                        w.append(yem_all[(sl - 1, PREVK[k])])
                    if k in (4, 5):
                        kk2 = FCS * TTN - 2 + (k - 4)
                        w.append((("sc", hsm[kk2]) if L == 0
                                  else ("ev", hvm[kk2])))
                    w.append((("sc", hsm[(FCS - 1) * TTN + tt]) if L == 0
                              else ("ev", hvm[(FCS - 1) * TTN + tt])))
                    for fc in range(FCS):
                        ym[k] = q.op("tensor", lambda t, fc=fc, do=do, tt=tt,
                                     ps=ps: t.matmul(
                            ps, arn(c, 8192 + fc * 1024 + do * 128, 128),
                            hsc(fc, tt), start=(fc == 0), stop=(fc == FCS - 1)),
                            waits=(w if fc == 0 else ()),
                            inc=("mm" if fc == FCS - 1 else None))
                    prev_tile = yem_all.get((sl - 1, k))
                    if e < 8:
                        ww = [("mm", ym[k])]
                        if k == 0:
                            ww.append(("sc", cbcm[e]))
                        ysl = 2048 + (k % 2) * 1536
                        if k >= 2:
                            ww.append(yem[k - 2])
                        mq = q.op("vector", lambda v, tt=tt, ps=ps, ysl=ysl:
                                  v.tensor_tensor(
                            out=stg(c, ysl, 512), in0=ps,
                            in1=CBC[:, tt * NT: tt * NT + NT], op=ALU.mult),
                            waits=ww)
                        gm = q.op("vector", lambda v, do=do, tt=tt, ysl=ysl:
                                  v.tensor_tensor(
                            out=at(c, do, tt), in0=at(c, do, tt),
                            in1=stg(c, ysl, 512), op=ALU.add),
                            waits=[prev_tile])
                        yem[k] = ("ev", gm)
                    else:
                        vm = q.op("vector", lambda v, do=do, tt=tt, ps=ps:
                                  v.tensor_tensor(
                            out=at(c, do, tt), in0=at(c, do, tt), in1=ps,
                            op=ALU.add), waits=[("mm", ym[k]), prev_tile])
                        yem[k] = ("ev", vm)
                    yem_all[(sl, k)] = yem[k]
            p1last[sl] = hm[FCS * TTN - 1]
            p2last[sl] = ym[DCH * TTN - 1]
    q.play(f"experts{L}")




PH = {
    "init": ph_init,
    "loadx": ph_load_x,
    "ap0": lambda c: ph_attn_pre(c, 0),
    "am0": lambda c: ph_attn_main(c, 0),
    "aj0": lambda c: ph_attn_proj(c, 0),
    "ap1": lambda c: ph_attn_pre(c, 1),
    "am1": lambda c: ph_attn_main(c, 1),
    "aj1": lambda c: ph_attn_proj(c, 1),
    "ln": ph_ln,
    "lnf": lambda c: ph_ln(c, write_xb=False),
    "g0": lambda c: ph_gate(c, 0),
    "g1": lambda c: ph_gate(c, 1),
    "b2c": ph_b2c,
    "ex0": lambda c: ph_experts(c, 0),
    "ex1": lambda c: ph_experts(c, 1),
    "store": ph_store_stream,
}

FULL = ["init", "loadx", "am0", "aj0", "ln", "g0", "b2c", "ex0", "ln",
        "am1", "aj1", "ln", "g1", "ex1", "lnf", "store"]


def build_program(phases):
    nc = bass.Bass("TRN2", target_bir_lowering=False)
    c = make_ctx(nc)
    c.sem = {}
    with contextlib.ExitStack() as es:
        for name in ["ld0", "ld1", "ld2", "ld3", "st0", "st1", "sw0", "sw1", "sw2", "sw3", "mm", "sc", "ev", "gp", "cc", "sf"]:
            c.sem[name] = Sem(es.enter_context(nc.semaphore(name)))
        for p in phases:
            PH[p](c)
    return nc, c


def make_inmaps(c, inputs, hp=None, streams=None):
    if hp is None:
        hp = host_prep(inputs)
    maps = []
    for core in range(8):
        b, half = core // 2, core % 2
        m = {}
        for n in c.input_names:
            if n == "x_fm_slice":
                m[n] = np.ascontiguousarray(
                    hp["x_fm"][b][:, half * TOK:(half + 1) * TOK])
            elif n == "thr":
                m[n] = core_thr(core)
            elif n == "stream_in":
                m[n] = streams[core]
            else:
                m[n] = hp[n]
        maps.append(m)
    return maps


def kernel(**inputs):
    hp = host_prep(inputs)
    nc, c = build_program(FULL)
    maps = make_inmaps(c, inputs, hp)
    res = run_bass_kernel_spmd(nc, maps, core_ids=list(range(8)))
    out = np.zeros((4, SEQ, D), np.float32)
    for core in range(8):
        b, half = core // 2, core % 2
        shard = np.asarray(res.results[core]["stream_out"])
        out[b, half * TOK:(half + 1) * TOK, :] = shard.T
    return out
